# version 10
# baseline (speedup 1.0000x reference)
import math
import contextlib
import numpy as np

import concourse.bass as bass
import concourse.mybir as mybir
from concourse.ap import AP
from concourse.bass_utils import run_bass_kernel_spmd

F32 = mybir.dt.float32
BF16 = mybir.dt.bfloat16
AF = mybir.ActivationFunctionType
OP = mybir.AluOpType
AX = mybir.AxisListType.X

DIM = 1024
HID = 2048
QK = 128
GRP = 256
NBUCK = 32
MAXD = 128
BSCALE = float(QK) ** 0.5
EPS = 1e-5
NCORES = 8
TOK = 2048
NTC = 16
NGRP = 8
KD = 8
NHC = 16
NEG = -1e9

DEBUG = {}


def _bucket_of(n):
    max_exact = NBUCK // 2
    ln_ratio = math.log(MAXD / max_exact)
    if n < max_exact:
        return n
    v = max_exact + int(math.log(max(n, 1) / max_exact) / ln_ratio * (NBUCK - max_exact))
    return min(v, NBUCK - 1)


def _host_consts():
    oh = np.zeros((NBUCK, GRP), dtype=np.float32)
    for d in range(GRP):
        oh[_bucket_of(d), d] = 1.0
    return oh


class Sem:
    def __init__(self, handle):
        self.h = handle
        self.n = 0


class Ctx:

    def __init__(self, cur, engine):
        self.cur = cur
        self.e = engine

    def op(self, name, f, inc=None, incn=1):
        if inc is not None:
            inc.n += incn
        if self.cur == name:
            inst = f(self.e)
            if inc is not None:
                inst.then_inc(inc.h, incn)

    def wait(self, name, sem, val):
        assert val <= sem.n, f"wait for {val} but ledger at {sem.n}"
        if self.cur == name:
            self.e.wait_ge(sem.h, val)


SB_BASE = 18304


def _sbuf_map():
    m = {}
    cur = SB_BASE

    def a(name, nbytes, align=32):
        nonlocal cur
        cur = (cur + align - 1) // align * align
        m[name] = cur
        cur += nbytes

    a("lnw", 32); a("lnb", 32)
    a("bqk", 4)
    a("qkw", 16); a("qkb", 16)
    a("qkw3s", 4); a("qkb3s", 4)
    a("bhg", 64)
    a("wgf", 256)
    a("wghF", 128); a("wglF", 128)
    a("eps", 4)
    a("maskb", 32)
    a("relf", 4); a("relhi", 2); a("rello", 2)
    a("BT", 2048)
    a("logits", 512)
    a("wtm", 576)
    a("stats", 384)
    a("wqk", 2048)
    a("xnlo", 4096)
    a("rings", 3584)
    a("xnF", 32768)
    a("ACC", 65536)
    a("normF", 32768)
    m["BIG"] = (cur + 31) // 32 * 32
    m["_big_size"] = 229376 - m["BIG"]
    assert m["_big_size"] >= 64 * 1024, f"BIG too small: {m['_big_size']}"
    return m


DEBUG_TAPS = {
    "wtm": ((128, 144), F32),
    "kvacc": ((128, HID), F32),
    "kvbase": ((128, HID), F32),
    "ACC_ffn": ((128, NTC * 1024), F32),
    "xnF": ((128, NTC * 1024), BF16),
    "normF": ((128, KD * TOK), BF16),
    "v0": ((128, HID), BF16),
    "logits": ((128, 128), F32),
    "BT": ((128, 512), F32),
    "attnT0": ((128, 512), BF16),
    "ACC_fin": ((128, NTC * 1024), F32),
    "x0": ((128, 1024), F32),
    "xn0": ((128, 1024), F32),
    "xbf0": ((128, 1024), BF16),
    "xpx0": ((128, 1024), F32),
    "xpx9": ((128, 1024), F32),
}


def build_program(n_debug=()):
    nc = bass.Bass("TRN2", target_bir_lowering=False, debug=False)

    d_x = nc.dram_tensor("x", [TOK, DIM], F32, kind="ExternalInput").ap()
    d_lnw = nc.dram_tensor("ln_w", [DIM], F32, kind="ExternalInput").ap()
    d_lnb = nc.dram_tensor("ln_b", [DIM], F32, kind="ExternalInput").ap()
    d_Wh = nc.dram_tensor("Wh", [DIM, 2 * HID], F32, kind="ExternalInput").ap()
    d_bh = nc.dram_tensor("bh", [2 * HID], F32, kind="ExternalInput").ap()
    d_Wqk = nc.dram_tensor("Wqk", [DIM, QK], F32, kind="ExternalInput").ap()
    d_bqk = nc.dram_tensor("bqk", [QK], F32, kind="ExternalInput").ap()
    d_qkw = nc.dram_tensor("qk_w", [4, QK], F32, kind="ExternalInput").ap()
    d_qkb = nc.dram_tensor("qk_b", [4, QK], F32, kind="ExternalInput").ap()
    d_rel = nc.dram_tensor("rel_table", [NBUCK], F32, kind="ExternalInput").ap()
    d_Wout = nc.dram_tensor("Wout", [HID, DIM], F32, kind="ExternalInput").ap()
    d_bout = nc.dram_tensor("bout", [DIM], F32, kind="ExternalInput").ap()
    d_Wg = nc.dram_tensor("Wg", [DIM, 8], F32, kind="ExternalInput").ap()
    d_Ws1 = nc.dram_tensor("Ws1", [DIM, HID], F32, kind="ExternalInput").ap()
    d_Ws3 = nc.dram_tensor("Ws3", [DIM, HID], F32, kind="ExternalInput").ap()
    d_Ws2 = nc.dram_tensor("Ws2", [HID, DIM], F32, kind="ExternalInput").ap()
    d_We1 = nc.dram_tensor("We1", [8, DIM, HID], F32, kind="ExternalInput").ap()
    d_We3 = nc.dram_tensor("We3", [8, DIM, HID], F32, kind="ExternalInput").ap()
    d_We2 = nc.dram_tensor("We2", [8, HID, DIM], F32, kind="ExternalInput").ap()
    d_oh = nc.dram_tensor("ohm", [NBUCK, GRP], F32, kind="ExternalInput").ap()
    d_sm = nc.dram_tensor("smask", [1, 8], F32, kind="ExternalInput").ap()
    d_out = nc.dram_tensor("out", [TOK, DIM], F32, kind="ExternalOutput").ap()

    d_text = nc.dram_tensor("t_ext", [512], F32).ap()
    d_xpx = nc.dram_tensor("xpx_spill", [NTC, 128, DIM], F32).ap()
    d_vsp = nc.dram_tensor("v_spill", [NTC, 128, HID], BF16).ap()
    d_nsp = nc.dram_tensor("n_spill", [NGRP, 128, KD * GRP], BF16).ap()
    d_ccin = nc.dram_tensor("cc_in", [QK, HID], F32).ap()
    d_ccout = nc.dram_tensor("cc_out", [NCORES * QK, HID], F32).ap()

    d_dbg = {}
    for nm in n_debug:
        shp, dt = DEBUG_TAPS[nm]
        d_dbg[nm] = nc.dram_tensor("dbg_" + nm, list(shp), dt, kind="ExternalOutput").ap()

    M = _sbuf_map()

    def sb(name, shape, dtype, off=None):
        off = M[name] if off is None else off
        return nc.alloc_sbuf_tensor_at(name, list(shape), dtype, offset=off).ap()

    lnw = sb("lnw", [128, 8], F32); lnb = sb("lnb", [128, 8], F32)
    bqk = sb("bqk", [128, 1], F32)
    qkw = sb("qkw", [128, 4], F32); qkb = sb("qkb", [128, 4], F32)
    qkw3s = sb("qkw3s", [128, 1], F32); qkb3s = sb("qkb3s", [128, 1], F32)
    bhg = sb("bhg", [128, 16], F32)
    wgf = sb("wgf", [128, 64], F32)
    wghF = sb("wghF", [128, 64], BF16)
    wglF = sb("wglF", [128, 64], BF16)
    epsc = sb("eps", [128, 1], F32)
    maskb = sb("maskb", [128, 8], F32)
    relf = sb("relf", [32, 1], F32)
    relhi = sb("relhi", [32, 1], BF16); rello = sb("rello", [32, 1], BF16)
    BT = sb("BT", [128, 512], F32)
    logits = sb("logits", [128, 128], F32)
    wtm = sb("wtm", [128, 144], F32)
    stats = sb("stats", [128, 96], F32)
    wqk = sb("wqk", [128, 1024], BF16)
    xnlo = sb("xnlo", [128, 2048], BF16)

    R = M["rings"]
    ohb = sb("ohb", [32, 256], BF16, off=R)
    trow = sb("trow", [1, 256], F32, off=R + 512)
    negrow = sb("negrow", [1, 256], F32, off=R + 1536)
    qks = sb("qks", [128, 256], F32, off=R)
    lkf = sb("lkf", [128, 256], BF16, off=R + 1024)
    linktm = sb("linktm", [128, 256], BF16, off=R + 1536)
    quadq = sb("quadq", [128, 256], BF16, off=R + 2048)
    quadk = sb("quadk", [128, 256], BF16, off=R + 2560)
    linq = sb("linq", [128, 256], BF16, off=R + 3072)
    sA = sb("sA", [128, 512], F32, off=R)
    pf = sb("pf", [128, 512], BF16, off=R + 2048)

    xnF = sb("xnF", [128, NTC * 1024], BF16)
    ACC = sb("ACC", [128, NTC * 1024], F32)
    normF = sb("normF", [128, KD * TOK], BF16)

    BIG = M["BIG"]
    xb = sb("xb", [128, 2048], F32, off=BIG)
    xcb = sb("xcb", [128, 2048], F32, off=BIG + 8192)
    sqb = sb("sqb", [128, 1024], F32, off=BIG + 16384)
    xnb = sb("xnb", [128, 2048], F32, off=BIG + 20480)
    xpxb = sb("xpxb", [128, 2048], F32, off=BIG + 28672)
    nrawb = sb("nrawb", [128, 2048], BF16, off=BIG + 36864)
    xnbf = sb("xnbf", [128, 2048], BF16, off=BIG + 40960)
    xnlob = sb("xnlob", [128, 2048], BF16, off=BIG + 45056)
    boutbc = sb("boutbc", [128, 1024], F32, off=BIG + 49152)

    wset = [
        {"w1": sb("w1_0", [128, 8192], BF16, off=BIG),
         "w3": sb("w3_0", [128, 8192], BF16, off=BIG + 16384),
         "w2": sb("w2_0", [128, 8192], BF16, off=BIG + 32768)},
        {"w1": sb("w1_1", [128, 8192], BF16, off=BIG + 49152),
         "w3": sb("w3_1", [128, 8192], BF16, off=M["normF"]),
         "w2": sb("w2_1", [128, 8192], BF16, off=M["normF"] + 16384)},
    ]
    gate_r = sb("gate_r", [128, 512], F32, off=BIG + 49152)
    outh_r = sb("outh_r", [128, 512], BF16, off=BIG + 51200)
    attnT = sb("attnT", [128, 512], BF16, off=BIG + 52224)
    simy = sb("simy", [128, 512], F32, off=BIG + 53248)
    fill = sb("fill", [128, 256], F32, off=BIG + 55296)
    xnlo_b = sb("xnlo_b", [128, 2048], BF16, off=BIG + 56320)

    AC = M["ACC"]
    whv = sb("whv", [128, KD * HID], BF16, off=AC)
    bhvbc = sb("bhvbc", [128, HID], F32, off=AC + 32768)
    vbufA = sb("vbufA", [128, 2 * HID], BF16, off=AC + 40960)
    kvaccA = sb("kvaccA", [128, HID], F32, off=AC + 49152)
    vtmp = sb("vtmp", [128, 1024], F32, off=AC + 57344)

    wout_sb = sb("wout_sb", [128, NHC * 1024], BF16, off=BIG)
    vring = sb("vring", [128, 4 * HID], BF16, off=BIG + 32768)
    whg_sb = sb("whg_sb", [128, KD * HID], BF16, off=M["xnF"])
    normg = sb("normg", [128, 2 * KD * GRP], BF16, off=M["normF"])
    kvrun = sb("kvrun", [128, HID], F32, off=M["normF"] + 8192)
    kvbf = sb("kvbf", [128, HID], BF16, off=M["normF"] + 16384)
    ccr = sb("ccr", [128, HID], F32, off=M["normF"] + 20480)

    PS = [nc.alloc_psum_tensor(f"ps{i}", [128, 512], F32).ap() for i in range(8)]

    es = contextlib.ExitStack()
    sem_names = ["ld", "sp", "sc", "pe", "act", "dve", "cc", "out"]
    handles = {nm: es.enter_context(nc.semaphore("s_" + nm)) for nm in sem_names}
    block = es.enter_context(nc.Block())

    def program(ctx):
        S = {nm: Sem(handles[nm]) for nm in sem_names}
        ld, sp, sc, pe, act, dve, cc, out_s = (
            S["ld"], S["sp"], S["sc"], S["pe"], S["act"], S["dve"], S["cc"], S["out"])

        def dbg_dump(name, src_ap):
            if name in d_dbg:
                ctx.op("gpsimd", lambda g: g.dma_start(out=d_dbg[name], in_=src_ap), sp, 16)

        ctx.op("sync", lambda e: e.dma_start(out=relf, in_=d_rel[:, None]), ld, 16)
        ctx.op("sync", lambda e: e.dma_start(out=bqk, in_=d_bqk[:, None]), ld, 16)
        for h in range(4):
            ctx.op("sync", lambda e, h=h: e.dma_start(out=qkw[:, h:h + 1], in_=d_qkw[h, :][:, None]), ld, 16)
            ctx.op("sync", lambda e, h=h: e.dma_start(out=qkb[:, h:h + 1], in_=d_qkb[h, :][:, None]), ld, 16)
        for c in range(8):
            ctx.op("sync", lambda e, c=c: e.dma_start(out=lnw[:, c:c + 1], in_=d_lnw[c * 128:(c + 1) * 128][:, None]), ld, 16)
            ctx.op("sync", lambda e, c=c: e.dma_start(out=lnb[:, c:c + 1], in_=d_lnb[c * 128:(c + 1) * 128][:, None]), ld, 16)
        for c in range(16):
            ctx.op("sync", lambda e, c=c: e.dma_start(out=bhg[:, c:c + 1], in_=d_bh[HID + c * 128: HID + (c + 1) * 128][:, None]), ld, 16)
        ctx.op("sync", lambda e: e.dma_start(
            out=wgf.rearrange("p (c n) -> p c n", n=8),
            in_=d_Wg.rearrange("(c p) n -> p c n", p=128)), ld, 16)
        ld_pro = ld.n

        ctx.op("gpsimd", lambda g: g.dma_start(out=ohb, in_=d_oh), sp, 16)
        ctx.op("gpsimd", lambda g: g.dma_start(out=maskb, in_=d_sm.broadcast_to([128, 8])), sp, 16)
        ctx.op("gpsimd", lambda g: g.dma_start(out=boutbc, in_=d_bout[None, :].broadcast_to([128, DIM])), sp, 16)
        ctx.op("gpsimd", lambda g: g.dma_start(out=bhvbc, in_=d_bh[None, 0:HID].broadcast_to([128, HID])), sp, 16)
        sp_bc = sp.n
        ctx.op("gpsimd", lambda g: g.dma_start(
            out=wqk.rearrange("p (c n) -> p c n", n=128),
            in_=d_Wqk.rearrange("(c p) n -> p c n", p=128)), sp, 16)
        ctx.op("gpsimd", lambda g: g.dma_start(
            out=whv.rearrange("p (c n) -> p c n", n=HID),
            in_=d_Wh.rearrange("(c p) n -> p c n", p=128)[:, :, 0:HID]), sp, 16)
        sp_wts = sp.n

        ctx.wait("vector", ld, ld_pro)
        ctx.wait("vector", sp, sp_bc)
        ctx.op("vector", lambda v: v.memset(epsc, EPS))
        ctx.op("vector", lambda v: v.memset(negrow, NEG))
        ctx.op("vector", lambda v: v.memset(kvaccA, 0.0))
        ctx.op("vector", lambda v: v.tensor_scalar(out=qkw3s, in0=qkw[:, 3:4], scalar1=1.0 / GRP, scalar2=None, op0=OP.mult))
        ctx.op("vector", lambda v: v.tensor_scalar(out=qkb3s, in0=qkb[:, 3:4], scalar1=1.0 / GRP, scalar2=None, op0=OP.mult))
        ctx.op("vector", lambda v: v.tensor_scalar(out=relf, in0=relf, scalar1=BSCALE, scalar2=None, op0=OP.mult))
        ctx.op("vector", lambda v: v.tensor_copy(out=wghF, in_=wgf))
        ctx.op("vector", lambda v: v.tensor_copy(out=relhi, in_=relf))
        ctx.op("vector", lambda v: v.tensor_sub(out=wglF, in0=wgf, in1=wghF))
        ctx.op("vector", lambda v: v.tensor_sub(out=rello, in0=relf, in1=relhi))
        ctx.op("vector", lambda v: v.tensor_copy(out=fill[:, 0:128], in_=fill[:, 128:256]), dve, 1)
        dve_pro = dve.n

        ctx.wait("tensor", dve, dve_pro)
        ctx.op("tensor", lambda t: t.matmul(out=PS[7][0:1, 0:256], lhsT=relhi, rhs=ohb, start=True, stop=False), pe, 1)
        ctx.op("tensor", lambda t: t.matmul(out=PS[7][0:1, 0:256], lhsT=rello, rhs=ohb, start=False, stop=True), pe, 1)
        pe_bt = pe.n
        ctx.wait("vector", pe, pe_bt)
        ctx.op("vector", lambda v: v.tensor_copy(out=trow, in_=PS[7][0:1, 0:256]))
        ctx.op("vector", lambda v: v.tensor_copy(out=fill[:, 0:128], in_=fill[:, 128:256]), dve, 1)
        dve_bt = dve.n
        ctx.wait("gpsimd", dve, dve_bt)
        ctx.op("gpsimd", lambda g: g.dma_start(out=d_text[None, 0:256], in_=negrow), sp, 16)
        ctx.op("gpsimd", lambda g: g.dma_start(out=d_text[None, 256:512], in_=trow), sp, 16)
        sp_text = sp.n

        for tc0 in range(2):
            ctx.op("sync", lambda e, tc=tc0: e.dma_start(
                out=xb[:, (tc % 2) * 1024:(tc % 2) * 1024 + 1024],
                in_=d_x[tc * 128:(tc + 1) * 128, :]), ld, 16)
        ld_x = [ld_pro + 16, ld_pro + 32]

        act_sqrt2 = []
        dve_xpx_l = []
        dve_ln = []
        dve_nsc = []
        dve_rtr_evac = []
        pe_rtr = []
        sc_tr = []
        sp_xpx = []
        act_vev = []
        sp_vsp = []
        dve_kv = []
        dve_vt = []
        state = {"dve_lkf_last": dve_pro, "sc_lkt": None, "pe_kv_last": None,
                 "act_qk_last": None}

        def spc(v):
            return v.tensor_copy(out=fill[:, 0:128], in_=fill[:, 128:256])

        def xnlo_slot(tc):
            s4 = tc % 4
            base = xnlo if s4 < 2 else xnlo_b
            return base[:, (s4 % 2) * 1024:(s4 % 2) * 1024 + 1024]

        def emit_ln(tc):
            X = xb[:, (tc % 2) * 1024:(tc % 2) * 1024 + 1024]
            XN = xnb[:, (tc % 2) * 1024:(tc % 2) * 1024 + 1024]
            XP = xpxb[:, (tc % 2) * 1024:(tc % 2) * 1024 + 1024]
            NR = nrawb[:, (tc % 2) * 1024:(tc % 2) * 1024 + 1024]
            XBF = xnbf[:, (tc % 2) * 1024:(tc % 2) * 1024 + 1024]
            XLO = xnlob[:, (tc % 2) * 1024:(tc % 2) * 1024 + 1024]
            st = stats[:, (tc % 2) * 16:(tc % 2) * 16 + 16]

            def ln_stats(SRC, o):
                ctx.op("vector", lambda v: v.tensor_mul(out=sqb[:, 0:1024], in0=SRC, in1=SRC))
                ctx.op("vector", lambda v: v.reduce_sum(out=st[:, o:o + 1], in_=SRC, axis=AX))
                ctx.op("vector", lambda v: v.reduce_sum(out=st[:, o + 1:o + 2], in_=sqb[:, 0:1024], axis=AX))
                ctx.op("vector", lambda v: v.tensor_scalar(out=st[:, o:o + 1], in0=st[:, o:o + 1], scalar1=1.0 / DIM, scalar2=None, op0=OP.mult))
                ctx.op("vector", lambda v: v.tensor_scalar(out=st[:, o + 1:o + 2], in0=st[:, o + 1:o + 2], scalar1=1.0 / DIM, scalar2=None, op0=OP.mult))
                ctx.op("vector", lambda v: v.tensor_mul(out=st[:, o + 2:o + 3], in0=st[:, o:o + 1], in1=st[:, o:o + 1]))
                ctx.op("vector", spc)
                ctx.op("vector", lambda v: v.tensor_sub(out=st[:, o + 3:o + 4], in0=st[:, o + 1:o + 2], in1=st[:, o + 2:o + 3]))
                ctx.op("vector", spc, dve, 1)
                ctx.wait("scalar", dve, dve.n)
                ctx.op("scalar", lambda s: s.nop(cycle_cnt=800))
                ctx.op("scalar", lambda s: s.activation(out=st[:, o + 4:o + 5], in_=st[:, o + 3:o + 4], func=AF.Sqrt, bias=epsc[:, 0:1]), act, 1)
                ctx.wait("vector", act, act.n)
                ctx.op("vector", lambda v: v.nop(cycle_cnt=800))
                ctx.op("vector", lambda v: v.reciprocal(out=st[:, o + 5:o + 6], in_=st[:, o + 4:o + 5]))
                ctx.op("vector", spc)

            ctx.wait("vector", ld, ld_x[tc])
            if tc >= 2:
                ctx.wait("vector", sc, sc_tr[tc - 2])
                ctx.wait("vector", act, act_sqrt2[tc - 2])
                ctx.wait("vector", sp, sp_xpx[tc - 2])
            ln_stats(X, 0)
            ctx.op("vector", lambda v: v.tensor_scalar(out=XN, in0=X, scalar1=st[:, 0:1], scalar2=st[:, 5:6], op0=OP.subtract, op1=OP.mult))
            ctx.op("vector", lambda v: v.tensor_add(out=XP, in0=X, in1=XN))
            ctx.op("vector", lambda v: v.tensor_add(out=XP, in0=XP, in1=boutbc), dve, 1)
            dve_xpx_l.append(dve.n)
            ln_stats(XN, 8)
            act_sqrt2.append(act.n)
            ctx.op("vector", lambda v: v.tensor_scalar(out=NR, in0=XN, scalar1=st[:, 8:9], scalar2=st[:, 13:14], op0=OP.subtract, op1=OP.mult))
            ctx.op("vector", lambda v: v.tensor_copy(out=XBF, in_=XN))
            ctx.op("vector", lambda v: v.tensor_sub(out=XLO, in0=XN, in1=XBF), dve, 1)
            dve_ln.append(dve.n)

            ctx.wait("gpsimd", dve, dve_xpx_l[-1])
            ctx.op("gpsimd", lambda g_, tc=tc, XP=XP: g_.dma_start(out=d_xpx[tc], in_=XP), sp, 16)
            sp_xpx.append(sp.n)

            if tc + 2 < NTC:
                ctx.wait("sync", dve, dve_xpx_l[-1])
                ctx.op("sync", lambda e, tc=tc: e.dma_start(
                    out=xb[:, (tc % 2) * 1024:(tc % 2) * 1024 + 1024],
                    in_=d_x[(tc + 2) * 128:(tc + 3) * 128, :]), ld, 16)
                ld_x.append(ld.n)

            ctx.wait("sync", dve, dve_ln[-1])
            ctx.wait("scalar", dve, dve_ln[-1])
            if tc >= 4:
                ctx.wait("scalar", pe, pe_rtr[tc - 4])
            XLOD = xnlo_slot(tc)
            for c in range(8):
                ctx.op("sync", lambda e, c=c, tc=tc, XBF=XBF: e.dma_start(
                    out=xnF[:, c * TOK + tc * 128: c * TOK + tc * 128 + 128],
                    in_=XBF[:, c * 128:c * 128 + 128], transpose=True), sc, 16)
                ctx.op("scalar", lambda e, c=c, tc=tc, XLO=XLO, XLOD=XLOD: e.dma_start(
                    out=XLOD[:, c * 128:c * 128 + 128],
                    in_=XLO[:, c * 128:c * 128 + 128], transpose=True), sc, 16)
                ctx.op("sync", lambda e, c=c, tc=tc, NR=NR: e.dma_start(
                    out=normF[:, c * TOK + tc * 128: c * TOK + tc * 128 + 128],
                    in_=NR[:, c * 128:c * 128 + 128], transpose=True), sc, 16)
            sc_tr.append(sc.n)

            ctx.wait("vector", sc, sc_tr[-1])
            for c in range(8):
                ctx.op("vector", lambda v, c=c, tc=tc: v.tensor_scalar(
                    out=normF[:, c * TOK + tc * 128: c * TOK + tc * 128 + 128],
                    in0=normF[:, c * TOK + tc * 128: c * TOK + tc * 128 + 128],
                    scalar1=lnw[:, c:c + 1], scalar2=lnb[:, c:c + 1],
                    op0=OP.mult, op1=OP.add), dve, 1)
            dve_nsc.append(dve.n)

        def emit_router(tc):
            ctx.wait("tensor", sc, sc_tr[tc])
            ctx.wait("tensor", dve, dve_pro)
            if tc >= 1:
                ctx.wait("tensor", dve, dve_rtr_evac[tc - 1])
            if dve_kv:
                ctx.wait("tensor", dve, dve_kv[-1])
            for c in range(8):
                ctx.op("tensor", lambda t, c=c, tc=tc: t.matmul(
                    out=PS[6][:, 0:8], lhsT=xnF[:, c * TOK + tc * 128: c * TOK + tc * 128 + 128],
                    rhs=wghF[:, c * 8:c * 8 + 8], start=(c == 0), stop=False), pe, 1)
            for c in range(8):
                ctx.op("tensor", lambda t, c=c, tc=tc: t.matmul(
                    out=PS[6][:, 0:8], lhsT=xnF[:, c * TOK + tc * 128: c * TOK + tc * 128 + 128],
                    rhs=wglF[:, c * 8:c * 8 + 8], start=False, stop=False), pe, 1)
            XLOD = xnlo_slot(tc)
            for c in range(8):
                ctx.op("tensor", lambda t, c=c, tc=tc, XLOD=XLOD: t.matmul(
                    out=PS[6][:, 0:8], lhsT=XLOD[:, c * 128:c * 128 + 128],
                    rhs=wghF[:, c * 8:c * 8 + 8], start=False, stop=(c == 7)), pe, 1)
            pe_rtr.append(pe.n)
            ctx.wait("vector", pe, pe.n)
            ctx.op("vector", lambda v, tc=tc: v.tensor_copy(out=logits[:, tc * 8:tc * 8 + 8], in_=PS[6][:, 0:8]), dve, 1)
            dve_rtr_evac.append(dve.n)

        def emit_attnA(g):
            ctx.wait("tensor", dve, dve_nsc[2 * g + 1])
            ctx.wait("tensor", sp, sp_wts)
            if state["act_qk_last"] is not None:
                ctx.wait("tensor", act, state["act_qk_last"])
            for c in range(KD):
                ctx.op("tensor", lambda t, c=c, g=g: t.matmul(
                    out=PS[4][:, 0:256], lhsT=wqk[:, c * 128:c * 128 + 128],
                    rhs=normF[:, c * TOK + g * 256: c * TOK + g * 256 + 256],
                    start=(c == 0), stop=(c == KD - 1)), pe, 1)
            pe_qk = pe.n
            ctx.wait("scalar", pe, pe_qk)
            if g > 0:
                ctx.wait("scalar", dve, state["dve_lkf_last"])
                ctx.wait("scalar", sc, state["sc_lkt"])
            ctx.op("scalar", lambda s: s.activation(out=qks, in_=PS[4][:, 0:256], func=AF.Silu, bias=bqk[:, 0:1]), act, 1)
            act_qk = act.n
            state["act_qk_last"] = act_qk
            ctx.wait("vector", act, act_qk)
            if g > 0:
                ctx.wait("vector", sc, state["sc_lkt"])
            ctx.op("vector", lambda v: v.tensor_scalar(out=lkf, in0=qks, scalar1=qkw3s[:, 0:1], scalar2=qkb3s[:, 0:1], op0=OP.mult, op1=OP.add), dve, 1)
            dve_lkf = dve.n
            state["dve_lkf_last"] = dve_lkf
            ctx.wait("sync", dve, dve_lkf)
            if g > 0:
                ctx.wait("sync", pe, state["pe_kv_last"])
            for i in range(2):
                ctx.op("sync", lambda e, i=i: e.dma_start(
                    out=linktm[:, i * 128:i * 128 + 128],
                    in_=lkf[:, i * 128:i * 128 + 128], transpose=True), sc, 16)
            state["sc_lkt"] = sc.n

            for i in range(2):
                tc = 2 * g + i
                vb = vbufA[:, (tc % 2) * HID:(tc % 2) * HID + HID]
                for hb in range(4):
                    b = (i * 4 + hb) % 4
                    if len(dve_vt) >= 4:
                        ctx.wait("tensor", dve, dve_vt[-4])
                    if len(sp_vsp) >= 2 and hb == 0:
                        ctx.wait("tensor", sp, sp_vsp[-2])
                    for c in range(KD):
                        ctx.op("tensor", lambda t, c=c, tc=tc, hb=hb, b=b: t.matmul(
                            out=PS[b][:, 0:512],
                            lhsT=normF[:, c * TOK + tc * 128: c * TOK + tc * 128 + 128],
                            rhs=whv[:, c * HID + hb * 512: c * HID + hb * 512 + 512],
                            start=(c == 0), stop=(c == KD - 1)), pe, 1)
                    pe_v = pe.n
                    ctx.wait("vector", pe, pe_v)
                    if len(act_vev) >= 2:
                        ctx.wait("vector", act, act_vev[-2])
                    ctx.op("vector", lambda v, hb=hb, b=b: v.tensor_add(
                        out=vtmp[:, (hb % 2) * 512:(hb % 2) * 512 + 512],
                        in0=PS[b][:, 0:512], in1=bhvbc[:, hb * 512:hb * 512 + 512]), dve, 1)
                    dve_vt.append(dve.n)
                    ctx.wait("scalar", dve, dve_vt[-1])
                    ctx.op("scalar", lambda s, hb=hb, vb=vb: s.activation(
                        out=vb[:, hb * 512:hb * 512 + 512],
                        in_=vtmp[:, (hb % 2) * 512:(hb % 2) * 512 + 512], func=AF.Silu), act, 1)
                    act_vev.append(act.n)
                ctx.wait("gpsimd", act, act_vev[-1])
                ctx.op("gpsimd", lambda gp, tc=tc, vb=vb: gp.dma_start(out=d_vsp[tc], in_=vb), sp, 16)
                sp_vsp.append(sp.n)

            ctx.wait("tensor", sc, state["sc_lkt"])
            ctx.wait("tensor", act, act_vev[-1])
            if dve_kv:
                ctx.wait("tensor", dve, dve_kv[-1])
            if dve_rtr_evac:
                ctx.wait("tensor", dve, dve_rtr_evac[-1])
            kv_ev = []
            for hb in range(4):
                b = 5 + (hb % 2)
                if hb >= 2:
                    ctx.wait("tensor", dve, kv_ev[hb - 2])
                for i in range(2):
                    vb = vbufA[:, ((2 * g + i) % 2) * HID: ((2 * g + i) % 2) * HID + HID]
                    ctx.op("tensor", lambda t, i=i, hb=hb, b=b, vb=vb: t.matmul(
                        out=PS[b][:, 0:512], lhsT=linktm[:, i * 128:i * 128 + 128],
                        rhs=vb[:, hb * 512:hb * 512 + 512],
                        start=(i == 0), stop=(i == 1)), pe, 1)
                pe_kv = pe.n
                ctx.wait("vector", pe, pe_kv)
                ctx.op("vector", lambda v, hb=hb, b=b: v.tensor_add(
                    out=kvaccA[:, hb * 512:hb * 512 + 512], in0=PS[b][:, 0:512],
                    in1=kvaccA[:, hb * 512:hb * 512 + 512]), dve, 1)
                kv_ev.append(dve.n)
            dve_kv.append(dve.n)
            state["pe_kv_last"] = pe.n

        def emit_set_load(s):
            e, h = s // 2, s % 2
            slot = wset[s % 2]
            if e == 0:
                w1src = d_Ws1.rearrange("(c p) n -> p c n", p=128)[:, :, h * 1024:(h + 1) * 1024]
                w3src = d_Ws3.rearrange("(c p) n -> p c n", p=128)[:, :, h * 1024:(h + 1) * 1024]
                w2src = d_Ws2[h * 1024:(h + 1) * 1024, :].rearrange("(f p) n -> p f n", p=128)
            else:
                w1src = d_We1[e - 1].rearrange("(c p) n -> p c n", p=128)[:, :, h * 1024:(h + 1) * 1024]
                w3src = d_We3[e - 1].rearrange("(c p) n -> p c n", p=128)[:, :, h * 1024:(h + 1) * 1024]
                w2src = d_We2[e - 1][h * 1024:(h + 1) * 1024, :].rearrange("(f p) n -> p f n", p=128)
            ctx.op("gpsimd", lambda g: g.dma_start(
                out=slot["w1"].rearrange("p (c n) -> p c n", n=1024), in_=w1src), sp, 16)
            ctx.op("gpsimd", lambda g: g.dma_start(
                out=slot["w3"].rearrange("p (c n) -> p c n", n=1024), in_=w3src), sp, 16)
            ctx.op("gpsimd", lambda g: g.dma_start(
                out=slot["w2"].rearrange("p (f n) -> p f n", n=1024), in_=w2src), sp, 16)
            return sp.n

        sp_sets = []
        sp_nsp_l = []
        for g in range(NGRP):
            emit_ln(2 * g)
            emit_ln(2 * g + 1)
            ctx.wait("gpsimd", dve, dve_nsc[-1])
            for c in range(KD):
                ctx.op("gpsimd", lambda gp, g=g, c=c: gp.dma_start(
                    out=d_nsp[g][:, c * 256:c * 256 + 256],
                    in_=normF[:, c * TOK + g * 256:c * TOK + g * 256 + 256]), sp, 16)
            sp_nsp_l.append(sp.n)
            if g == NGRP - 1:
                sp_sets.append(emit_set_load(0))
            if g >= 1:
                emit_router(2 * g - 2)
                emit_router(2 * g - 1)
                emit_attnA(g - 1)
        emit_router(2 * NGRP - 2)
        emit_router(2 * NGRP - 1)
        emit_attnA(NGRP - 1)
        pe_a_end = pe.n
        dve_ln_done = dve.n
        sc_ln_done = sc.n
        ctx.wait("gpsimd", dve, dve_kv[-1])
        dbg_dump("kvacc", kvaccA)
        dbg_dump("xnF", xnF)
        dbg_dump("normF", normF)

        ctx.wait("sync", sp, sp_text)
        for K in range(256):
            kc, k = K // 128, K % 128
            src = AP(tensor=d_text.tensor, offset=256 - K, ap=[[1, 1], [1, 256]])
            ctx.op("sync", lambda e, kc=kc, k=k, src=src: e.dma_start(
                out=BT[k:k + 1, kc * 256:(kc + 1) * 256], in_=src), sc, 16)
        sc_bt = sc.n
        ctx.wait("gpsimd", sc, sc_bt)
        dbg_dump("BT", BT)

        ctx.wait("scalar", dve, dve_rtr_evac[-1])
        ctx.op("scalar", lambda s: s.activation(out=logits, in_=logits, func=AF.Sigmoid), act, 1)
        act_sig = act.n
        ctx.wait("vector", act, act_sig)
        for pt in range(NTC // 2):
            els = []
            for j in range(2):
                tc = 2 * pt + j
                base = 32 + 32 * j
                els.append(dict(
                    g8=logits[:, tc * 8:tc * 8 + 8],
                    m1=stats[:, base:base + 1], m2=stats[:, base + 1:base + 2],
                    s12=stats[:, base + 2:base + 3],
                    E=stats[:, base + 4:base + 12], G2=stats[:, base + 12:base + 20],
                    w=wtm[:, tc * 9 + 1: tc * 9 + 9]))
            for E_ in els:
                ctx.op("vector", lambda v, e=E_: v.reduce_max(out=e["m1"], in_=e["g8"], axis=AX))
            for E_ in els:
                ctx.op("vector", lambda v, e=E_: v.tensor_scalar(out=e["E"], in0=e["g8"], scalar1=e["m1"], scalar2=None, op0=OP.is_equal))
            for E_ in els:
                ctx.op("vector", lambda v, e=E_: v.scalar_tensor_tensor(out=e["G2"], in0=e["E"], scalar=NEG, in1=e["g8"], op0=OP.mult, op1=OP.add))
            for E_ in els:
                ctx.op("vector", lambda v, e=E_: v.reduce_max(out=e["m2"], in_=e["G2"], axis=AX))
            for E_ in els:
                ctx.op("vector", lambda v, e=E_: v.tensor_scalar(out=e["E"], in0=e["g8"], scalar1=e["m2"], scalar2=None, op0=OP.is_ge))
            for E_ in els:
                ctx.op("vector", lambda v, e=E_: v.tensor_add(out=e["s12"], in0=e["m1"], in1=e["m2"]))
            for E_ in els:
                ctx.op("vector", lambda v, e=E_: v.reciprocal(out=e["s12"], in_=e["s12"]))
            for E_ in els:
                ctx.op("vector", lambda v, e=E_: v.tensor_scalar(out=e["s12"], in0=e["s12"], scalar1=1.0 / 3.0, scalar2=None, op0=OP.mult))
            for E_ in els:
                ctx.op("vector", lambda v, e=E_: v.tensor_mul(out=e["E"], in0=e["E"], in1=e["g8"]))
            for E_ in els:
                ctx.op("vector", lambda v, e=E_: v.tensor_scalar(
                    out=e["w"], in0=e["E"], scalar1=e["s12"], scalar2=None, op0=OP.mult), dve, 1)
        dve_router = dve.n
        ctx.wait("gpsimd", dve, dve_router)
        dbg_dump("wtm", wtm)
        dbg_dump("logits", logits)

        ctx.wait("gpsimd", dve, dve_kv[-1])
        ctx.op("gpsimd", lambda g: g.dma_start(out=d_ccin, in_=kvaccA), sp, 16)
        sp_ccin = sp.n
        ctx.wait("gpsimd", sp, sp_ccin)
        ctx.op("gpsimd", lambda g: g.collective_compute(
            "AllGather", OP.bypass, replica_groups=[list(range(NCORES))],
            ins=[d_ccin.opt()], outs=[d_ccout.opt()]), cc, 1)

        sp_nsp = sp_nsp_l[-1]

        ctx.wait("gpsimd", pe, pe_a_end)
        for tc in range(NTC):
            ctx.op("gpsimd", lambda g, tc=tc: g.dma_start(
                out=ACC[:, tc * 1024:(tc + 1) * 1024], in_=d_xpx[tc]), sp, 16)
        sp_accinit = sp.n

        pe_sets = []
        dve_yev = []
        sp_wout = None
        sp_whg = None

        ctx.wait("gpsimd", sp, sp_nsp)
        ctx.wait("gpsimd", pe, pe_rtr[-1])
        sp_sets.append(emit_set_load(1))

        ctx.wait("vector", sp, sp_accinit)

        for s in range(18):
            e = s // 2
            slot = wset[s % 2]
            ctx.wait("tensor", sp, sp_sets[s])
            if s == 0:
                ctx.wait("tensor", dve, dve_kv[-1])
                ctx.wait("tensor", dve, dve_router)
            for strip in range(NGRP):
                pf_l = []
                silu_l = []

                def emit_y(f):
                    ctx.wait("tensor", dve, pf_l[f])
                    for tcl in range(2):
                        for oh2 in range(2):
                            ctx.op("tensor", lambda t, f=f, tcl=tcl, oh2=oh2: t.matmul(
                                out=PS[2 * tcl + oh2][:, 0:512],
                                lhsT=pf[:, (f % 2) * 256 + tcl * 128:(f % 2) * 256 + tcl * 128 + 128],
                                rhs=slot["w2"][:, f * 1024 + oh2 * 512: f * 1024 + oh2 * 512 + 512],
                                start=(f == 0), stop=(f == 7)), pe, 1)

                for f in range(8):
                    ab = 4 + 2 * (f % 2)
                    if f >= 2:
                        ctx.wait("tensor", act, silu_l[f - 2])
                        ctx.wait("tensor", dve, pf_l[f - 2])
                    for c in range(KD):
                        ctx.op("tensor", lambda t, c=c, f=f, ab=ab, strip=strip: t.matmul(
                            out=PS[ab][:, 0:256],
                            lhsT=slot["w1"][:, c * 1024 + f * 128: c * 1024 + f * 128 + 128],
                            rhs=xnF[:, c * TOK + strip * 256: c * TOK + strip * 256 + 256],
                            start=(c == 0), stop=(c == KD - 1)), pe, 1)
                    for c in range(KD):
                        ctx.op("tensor", lambda t, c=c, f=f, ab=ab, strip=strip: t.matmul(
                            out=PS[ab + 1][:, 0:256],
                            lhsT=slot["w3"][:, c * 1024 + f * 128: c * 1024 + f * 128 + 128],
                            rhs=xnF[:, c * TOK + strip * 256: c * TOK + strip * 256 + 256],
                            start=(c == 0), stop=(c == KD - 1)), pe, 1)
                    pe_ab = pe.n
                    ctx.wait("scalar", pe, pe_ab)
                    ctx.op("scalar", lambda sE, f=f, ab=ab: sE.activation(
                        out=sA[:, (f % 2) * 256:(f % 2) * 256 + 256], in_=PS[ab][:, 0:256], func=AF.Silu), act, 1)
                    silu_l.append(act.n)
                    ctx.wait("vector", act, act.n)
                    ctx.op("vector", lambda v, f=f, ab=ab: v.tensor_mul(
                        out=pf[:, (f % 2) * 256:(f % 2) * 256 + 256],
                        in0=sA[:, (f % 2) * 256:(f % 2) * 256 + 256], in1=PS[ab + 1][:, 0:256]), dve, 1)
                    pf_l.append(dve.n)
                    if f == 0 and dve_yev:
                        ctx.wait("tensor", dve, dve_yev[-1])
                    if f >= 1:
                        emit_y(f - 1)
                emit_y(7)
                pe_y = pe.n
                ctx.wait("vector", pe, pe_y)
                for tcl in range(2):
                    tcg = strip * 2 + tcl
                    for oh2 in range(2):
                        accsl = ACC[:, tcg * 1024 + oh2 * 512: tcg * 1024 + oh2 * 512 + 512]
                        if e == 0:
                            ctx.op("vector", lambda v, tcl=tcl, oh2=oh2, accsl=accsl: v.scalar_tensor_tensor(
                                out=accsl, in0=PS[2 * tcl + oh2][:, 0:512],
                                scalar=1.0 / 3.0, in1=accsl, op0=OP.mult, op1=OP.add), dve, 1)
                        else:
                            wcol = wtm[:, tcg * 9 + e: tcg * 9 + e + 1]
                            ctx.op("vector", lambda v, tcl=tcl, oh2=oh2, accsl=accsl, wcol=wcol: v.scalar_tensor_tensor(
                                out=accsl, in0=PS[2 * tcl + oh2][:, 0:512],
                                scalar=wcol, in1=accsl, op0=OP.mult, op1=OP.add), dve, 1)
                dve_yev.append(dve.n)
            pe_sets.append(pe.n)
            if s + 2 < 18:
                ctx.wait("gpsimd", pe, pe_sets[-1])
                sp_sets.append(emit_set_load(s + 2))
            elif s == 16:
                ctx.wait("gpsimd", pe, pe_sets[-1])
                ctx.op("gpsimd", lambda g: g.dma_start(
                    out=wout_sb.rearrange("p (c n) -> p c n", n=1024),
                    in_=d_Wout.rearrange("(c p) n -> p c n", p=128)), sp, 16)
                sp_wout = sp.n
            elif s == 17:
                ctx.wait("gpsimd", pe, pe_sets[-1])
                ctx.op("gpsimd", lambda g: g.dma_start(
                    out=whg_sb.rearrange("p (c n) -> p c n", n=HID),
                    in_=d_Wh.rearrange("(c p) n -> p c n", p=128)[:, :, HID:2 * HID]), sp, 16)
                sp_whg = sp.n
        ctx.wait("gpsimd", dve, dve_yev[-1])
        dbg_dump("ACC_ffn", ACC)

        ctx.wait("gpsimd", cc, 1)
        for r in range(NCORES):
            ctx.op("gpsimd", lambda g, r=r: g.dma_start(out=ccr, in_=d_ccout[r * 128:(r + 1) * 128, :]), sp, 16)
            sp_ccr = sp.n
            ctx.wait("vector", sp, sp_ccr)
            if r == 0:
                ctx.op("vector", lambda v: v.tensor_scalar(
                    out=kvrun, in0=ccr, scalar1=maskb[:, 0:1], scalar2=None, op0=OP.mult), dve, 1)
            else:
                ctx.op("vector", lambda v, r=r: v.scalar_tensor_tensor(
                    out=kvrun, in0=ccr, scalar=maskb[:, r:r + 1], in1=kvrun,
                    op0=OP.mult, op1=OP.add), dve, 1)
            ctx.wait("gpsimd", dve, dve.n)
        dve_base = dve.n
        dbg_dump("kvbase", kvrun)

        sp_vld = []
        sp_nld = []

        def emit_group_loads(g):
            for i in range(2):
                tc = 2 * g + i
                ctx.op("gpsimd", lambda gp, g=g, tc=tc, i=i: gp.dma_start(
                    out=vring[:, ((2 * g + i) % 4) * HID: ((2 * g + i) % 4) * HID + HID],
                    in_=d_vsp[tc]), sp, 16)
                sp_vld.append(sp.n)
            ctx.op("gpsimd", lambda gp, g=g: gp.dma_start(
                out=normg[:, (g % 2) * KD * 256:(g % 2) * KD * 256 + KD * 256], in_=d_nsp[g]), sp, 16)
            sp_nld.append(sp.n)

        ctx.wait("gpsimd", sp, sp_whg)
        emit_group_loads(0)
        emit_group_loads(1)

        pe_bgrp = []
        sc_lkt2_last = None
        for g in range(NGRP):
            nrm = normg[:, (g % 2) * KD * 256:(g % 2) * KD * 256 + KD * 256]
            ctx.wait("tensor", sp, sp_nld[g])
            ctx.wait("tensor", sp, sp_vld[2 * g + 1])
            if g == 0:
                ctx.wait("tensor", sp, sp_wout)
                ctx.wait("tensor", sp, sp_whg)
                ctx.wait("tensor", sc, sc_bt)
                ctx.wait("tensor", dve, dve_yev[-1])
                ctx.wait("tensor", dve, dve_base)

            if g >= 1:
                ctx.wait("vector", pe, pe_bgrp[g - 1])
                ctx.wait("vector", sc, sc_lkt2_last)
            ctx.wait("vector", dve, dve_base)
            ctx.op("vector", lambda v: v.tensor_copy(out=kvbf, in_=kvrun), dve, 1)

            for c in range(KD):
                ctx.op("tensor", lambda t, c=c, nrm=nrm: t.matmul(
                    out=PS[4][:, 0:256], lhsT=wqk[:, c * 128:c * 128 + 128],
                    rhs=nrm[:, c * 256:c * 256 + 256],
                    start=(c == 0), stop=(c == KD - 1)), pe, 1)
            pe_qk2 = pe.n
            ctx.wait("scalar", pe, pe_qk2)
            ctx.op("scalar", lambda s: s.activation(out=qks, in_=PS[4][:, 0:256], func=AF.Silu, bias=bqk[:, 0:1]), act, 1)
            act_qk2 = act.n
            ctx.wait("vector", act, act_qk2)
            for (dst, hh) in ((quadq, 0), (linq, 1), (quadk, 2)):
                ctx.op("vector", lambda v, dst=dst, hh=hh: v.tensor_scalar(
                    out=dst, in0=qks, scalar1=qkw[:, hh:hh + 1], scalar2=qkb[:, hh:hh + 1],
                    op0=OP.mult, op1=OP.add))
            ctx.op("vector", lambda v: v.tensor_scalar(out=lkf, in0=qks, scalar1=qkw3s[:, 0:1], scalar2=qkb3s[:, 0:1], op0=OP.mult, op1=OP.add), dve, 1)
            dve_qk2 = dve.n
            ctx.wait("sync", dve, dve_qk2)
            if g >= 1:
                ctx.wait("sync", pe, pe_bgrp[g - 1])
            for i in range(2):
                ctx.op("sync", lambda e, i=i: e.dma_start(
                    out=linktm[:, i * 128:i * 128 + 128],
                    in_=lkf[:, i * 128:i * 128 + 128], transpose=True), sc, 16)
            sc_lkt2 = sc.n
            sc_lkt2_last = sc_lkt2

            ctx.wait("tensor", sc, sc_lkt2)
            kv_ev2 = []
            for hb in range(4):
                b = 5 + (hb % 2)
                if hb >= 2:
                    ctx.wait("tensor", dve, kv_ev2[hb - 2])
                for i in range(2):
                    vb = vring[:, ((2 * g + i) % 4) * HID: ((2 * g + i) % 4) * HID + HID]
                    ctx.op("tensor", lambda t, i=i, hb=hb, b=b, vb=vb: t.matmul(
                        out=PS[b][:, 0:512], lhsT=linktm[:, i * 128:i * 128 + 128],
                        rhs=vb[:, hb * 512:hb * 512 + 512],
                        start=(i == 0), stop=(i == 1)), pe, 1)
                pe_kv2 = pe.n
                ctx.wait("vector", pe, pe_kv2)
                ctx.op("vector", lambda v, hb=hb, b=b: v.tensor_add(
                    out=kvrun[:, hb * 512:hb * 512 + 512], in0=PS[b][:, 0:512],
                    in1=kvrun[:, hb * 512:hb * 512 + 512]), dve, 1)
                kv_ev2.append(dve.n)
            dve_kv2 = dve.n

            ctx.wait("tensor", dve, dve_kv2)
            for kt in range(2):
                ctx.op("tensor", lambda t, kt=kt: t.matmul(
                    out=PS[4 + kt][:, 0:256], lhsT=quadk[:, kt * 128:kt * 128 + 128],
                    rhs=quadq, start=True, stop=True), pe, 1)
            pe_sim = pe.n
            ctx.wait("vector", pe, pe_sim)
            if g == 0:
                ctx.wait("vector", sc, sc_bt)
            for kt in range(2):
                ctx.op("vector", lambda v, kt=kt: v.scalar_tensor_tensor(
                    out=simy[:, kt * 256:kt * 256 + 256], in0=PS[4 + kt][:, 0:256],
                    scalar=1.0 / GRP, in1=BT[:, kt * 256:kt * 256 + 256],
                    op0=OP.mult, op1=OP.add))
            ctx.op("vector", lambda v: v.tensor_scalar(out=simy, in0=simy, scalar1=0.0, scalar2=None, op0=OP.max))
            ctx.op("vector", lambda v: v.tensor_mul(out=attnT, in0=simy, in1=simy), dve, 1)
            dve_attnT = dve.n
            if g == 0:
                ctx.wait("gpsimd", dve, dve_attnT)
                dbg_dump("attnT0", attnT)

            act_gate_l = []
            dve_outh_l = []
            pe_wout_l = []
            for hc in range(NHC):
                gb = 4 + (hc % 2)
                qb = 6 + (hc % 2)
                gsl = gate_r[:, (hc % 2) * 256:(hc % 2) * 256 + 256]
                osl = outh_r[:, (hc % 2) * 256:(hc % 2) * 256 + 256]
                if hc < 2:
                    ctx.wait("tensor", dve, dve_attnT)
                else:
                    ctx.wait("tensor", act, act_gate_l[hc - 2])
                for c in range(KD):
                    ctx.op("tensor", lambda t, c=c, hc=hc, gb=gb, nrm=nrm: t.matmul(
                        out=PS[gb][:, 0:256],
                        lhsT=whg_sb[:, c * HID + hc * 128: c * HID + hc * 128 + 128],
                        rhs=nrm[:, c * 256:c * 256 + 256],
                        start=(c == 0), stop=(c == KD - 1)), pe, 1)
                pe_gate = pe.n
                if hc < 2:
                    ctx.wait("tensor", dve, dve_kv2)
                else:
                    ctx.wait("tensor", dve, dve_outh_l[hc - 2])
                for kt in range(2):
                    vb = vring[:, ((2 * g + kt) % 4) * HID: ((2 * g + kt) % 4) * HID + HID]
                    ctx.op("tensor", lambda t, kt=kt, hc=hc, qb=qb, vb=vb: t.matmul(
                        out=PS[qb][:, 0:256], lhsT=vb[:, hc * 128:hc * 128 + 128],
                        rhs=attnT[:, kt * 256:kt * 256 + 256],
                        start=(kt == 0), stop=False), pe, 1)
                ctx.op("tensor", lambda t, hc=hc, qb=qb: t.matmul(
                    out=PS[qb][:, 0:256], lhsT=kvbf[:, hc * 128:hc * 128 + 128],
                    rhs=linq, start=False, stop=True), pe, 1)
                pe_ql = pe.n
                ctx.wait("scalar", pe, pe_gate)
                if hc >= 2:
                    ctx.wait("scalar", dve, dve_outh_l[hc - 2])
                ctx.op("scalar", lambda s, hc=hc, gb=gb, gsl=gsl: s.activation(
                    out=gsl, in_=PS[gb][:, 0:256], func=AF.Silu, bias=bhg[:, hc:hc + 1]), act, 1)
                act_gate_l.append(act.n)
                ctx.wait("vector", pe, pe_ql)
                ctx.wait("vector", act, act_gate_l[-1])
                if hc >= 2:
                    ctx.wait("vector", pe, pe_wout_l[hc - 2])
                ctx.op("vector", lambda v, qb=qb, gsl=gsl, osl=osl: v.tensor_mul(
                    out=osl, in0=gsl, in1=PS[qb][:, 0:256]), dve, 1)
                dve_outh_l.append(dve.n)
                ctx.wait("tensor", dve, dve_outh_l[-1])
                for tcl in range(2):
                    for oh2 in range(2):
                        ctx.op("tensor", lambda t, hc=hc, tcl=tcl, oh2=oh2, osl=osl: t.matmul(
                            out=PS[2 * tcl + oh2][:, 0:512],
                            lhsT=osl[:, tcl * 128:tcl * 128 + 128],
                            rhs=wout_sb[:, hc * 1024 + oh2 * 512: hc * 1024 + oh2 * 512 + 512],
                            start=(hc == 0), stop=(hc == NHC - 1)), pe, 1)
                pe_wout_l.append(pe.n)

            pe_grp = pe.n
            pe_bgrp.append(pe_grp)
            ctx.wait("vector", pe, pe_grp)
            for tcl in range(2):
                tcg = 2 * g + tcl
                for oh2 in range(2):
                    accsl = ACC[:, tcg * 1024 + oh2 * 512: tcg * 1024 + oh2 * 512 + 512]
                    ctx.op("vector", lambda v, tcl=tcl, oh2=oh2, accsl=accsl: v.tensor_add(
                        out=accsl, in0=PS[2 * tcl + oh2][:, 0:512], in1=accsl), dve, 1)
            dve_grp = dve.n

            if g + 2 < NGRP:
                ctx.wait("gpsimd", dve, dve_grp)
                emit_group_loads(g + 2)

            ctx.wait("sync", dve, dve_grp)
            for tcl in range(2):
                tcg = 2 * g + tcl
                ctx.op("sync", lambda e, tcg=tcg: e.dma_start(
                    out=d_out[tcg * 128:(tcg + 1) * 128, :],
                    in_=ACC[:, tcg * 1024:(tcg + 1) * 1024]), out_s, 16)

        if "ACC_fin" in d_dbg:
            ctx.wait("gpsimd", dve, dve.n)
            dbg_dump("ACC_fin", ACC)

        ctx.wait("sync", out_s, out_s.n)
        ctx.wait("sync", sp, sp.n)

    block.sync(lambda e: program(Ctx("sync", e)))
    block.scalar(lambda e: program(Ctx("scalar", e)))
    block.vector(lambda e: program(Ctx("vector", e)))
    block.tensor(lambda e: program(Ctx("tensor", e)))
    block.gpsimd(lambda e: program(Ctx("gpsimd", e)))

    es.close()
    return nc


def kernel(**inputs):
    oh = _host_consts()
    xf = np.ascontiguousarray(np.asarray(inputs["L0_attn_x"], dtype=np.float32).reshape(16384, DIM))
    base = {
        "ln_w": np.ascontiguousarray(inputs["L0_attn_ln_w"], dtype=np.float32),
        "ln_b": np.ascontiguousarray(inputs["L0_attn_ln_b"], dtype=np.float32),
        "Wh": np.ascontiguousarray(inputs["L0_attn_Wh"], dtype=np.float32),
        "bh": np.ascontiguousarray(inputs["L0_attn_bh"], dtype=np.float32),
        "Wqk": np.ascontiguousarray(inputs["L0_attn_Wqk"], dtype=np.float32),
        "bqk": np.ascontiguousarray(inputs["L0_attn_bqk"], dtype=np.float32),
        "qk_w": np.ascontiguousarray(inputs["L0_attn_qk_w"], dtype=np.float32),
        "qk_b": np.ascontiguousarray(inputs["L0_attn_qk_b"], dtype=np.float32),
        "rel_table": np.ascontiguousarray(inputs["L0_attn_rel_table"], dtype=np.float32),
        "Wout": np.ascontiguousarray(inputs["L0_attn_Wout"], dtype=np.float32),
        "bout": np.ascontiguousarray(inputs["L0_attn_bout"], dtype=np.float32),
        "Wg": np.ascontiguousarray(inputs["L0_ffn_Wg"], dtype=np.float32),
        "Ws1": np.ascontiguousarray(inputs["L0_ffn_Ws1"], dtype=np.float32),
        "Ws3": np.ascontiguousarray(inputs["L0_ffn_Ws3"], dtype=np.float32),
        "Ws2": np.ascontiguousarray(inputs["L0_ffn_Ws2"], dtype=np.float32),
        "We1": np.ascontiguousarray(inputs["L0_ffn_We1"], dtype=np.float32),
        "We3": np.ascontiguousarray(inputs["L0_ffn_We3"], dtype=np.float32),
        "We2": np.ascontiguousarray(inputs["L0_ffn_We2"], dtype=np.float32),
        "ohm": oh,
    }
    in_maps = []
    for c in range(NCORES):
        m = dict(base)
        m["x"] = np.ascontiguousarray(xf[c * TOK:(c + 1) * TOK])
        sm = np.zeros((1, 8), dtype=np.float32)
        for r in range(NCORES):
            if r // 4 == c // 4 and r < c:
                sm[0, r] = 1.0
        m["smask"] = sm
        in_maps.append(m)

    nc = build_program(n_debug=tuple(DEBUG.keys()))
    res = run_bass_kernel_spmd(nc, in_maps, core_ids=list(range(NCORES)))
    out = np.concatenate([np.asarray(r["out"]) for r in res.results], axis=0)
    if DEBUG:
        kernel.debug_results = res.results
    return out.reshape(2, 8192, DIM)


if __name__ == "__main__":
    nc = build_program()
    print("built ok")


# version 14 (final)
# speedup vs baseline: 1.0085x; 1.0085x over previous
import math
import contextlib
import numpy as np

import concourse.bass as bass
import concourse.mybir as mybir
from concourse.ap import AP
from concourse.bass_utils import run_bass_kernel_spmd

F32 = mybir.dt.float32
BF16 = mybir.dt.bfloat16
AF = mybir.ActivationFunctionType
OP = mybir.AluOpType
AX = mybir.AxisListType.X

DIM = 1024
HID = 2048
QK = 128
GRP = 256
NBUCK = 32
MAXD = 128
BSCALE = float(QK) ** 0.5
EPS = 1e-5
NCORES = 8
TOK = 2048
NTC = 16
NGRP = 8
KD = 8
NHC = 16
NEG = -1e9

DEBUG = {}


def _bucket_of(n):
    max_exact = NBUCK // 2
    ln_ratio = math.log(MAXD / max_exact)
    if n < max_exact:
        return n
    v = max_exact + int(math.log(max(n, 1) / max_exact) / ln_ratio * (NBUCK - max_exact))
    return min(v, NBUCK - 1)


def _host_consts():
    oh = np.zeros((NBUCK, GRP), dtype=np.float32)
    for d in range(GRP):
        oh[_bucket_of(d), d] = 1.0
    return oh


class Sem:
    def __init__(self, handle):
        self.h = handle
        self.n = 0


class Ctx:

    def __init__(self, cur, engine):
        self.cur = cur
        self.e = engine

    def op(self, name, f, inc=None, incn=1):
        if inc is not None:
            inc.n += incn
        if self.cur == name:
            inst = f(self.e)
            if inc is not None:
                inst.then_inc(inc.h, incn)

    def wait(self, name, sem, val):
        assert val <= sem.n, f"wait for {val} but ledger at {sem.n}"
        if self.cur == name:
            self.e.wait_ge(sem.h, val)


SB_BASE = 18304


def _sbuf_map():
    m = {}
    cur = SB_BASE

    def a(name, nbytes, align=32):
        nonlocal cur
        cur = (cur + align - 1) // align * align
        m[name] = cur
        cur += nbytes

    a("lnw", 32); a("lnb", 32)
    a("bqk", 4)
    a("qkw", 16); a("qkb", 16)
    a("qkw3s", 4); a("qkb3s", 4)
    a("bhg", 64)
    a("wgf", 256)
    a("wghF", 128); a("wglF", 128)
    a("eps", 4)
    a("maskb", 32)
    a("relf", 4); a("relhi", 2); a("rello", 2)
    a("BT", 2048)
    a("logits", 512)
    a("wtm", 576)
    a("stats", 384)
    a("wqk", 2048)
    a("xnlo", 4096)
    a("rings", 3584)
    a("xnF", 32768)
    a("ACC", 65536)
    a("normF", 32768)
    m["BIG"] = (cur + 31) // 32 * 32
    m["_big_size"] = 229376 - m["BIG"]
    assert m["_big_size"] >= 64 * 1024, f"BIG too small: {m['_big_size']}"
    return m


DEBUG_TAPS = {
    "wtm": ((128, 144), F32),
    "kvacc": ((128, HID), F32),
    "kvbase": ((128, HID), F32),
    "ACC_ffn": ((128, NTC * 1024), F32),
    "xnF": ((128, NTC * 1024), BF16),
    "normF": ((128, KD * TOK), BF16),
    "v0": ((128, HID), BF16),
    "logits": ((128, 128), F32),
    "BT": ((128, 512), F32),
    "attnT0": ((128, 512), BF16),
    "ACC_fin": ((128, NTC * 1024), F32),
    "x0": ((128, 1024), F32),
    "xn0": ((128, 1024), F32),
    "xbf0": ((128, 1024), BF16),
    "xpx0": ((128, 1024), F32),
    "xpx9": ((128, 1024), F32),
}


def build_program(n_debug=()):
    nc = bass.Bass("TRN2", target_bir_lowering=False, debug=False)

    d_x = nc.dram_tensor("x", [TOK, DIM], F32, kind="ExternalInput").ap()
    d_lnw = nc.dram_tensor("ln_w", [DIM], F32, kind="ExternalInput").ap()
    d_lnb = nc.dram_tensor("ln_b", [DIM], F32, kind="ExternalInput").ap()
    d_Wh = nc.dram_tensor("Wh", [DIM, 2 * HID], F32, kind="ExternalInput").ap()
    d_bh = nc.dram_tensor("bh", [2 * HID], F32, kind="ExternalInput").ap()
    d_Wqk = nc.dram_tensor("Wqk", [DIM, QK], F32, kind="ExternalInput").ap()
    d_bqk = nc.dram_tensor("bqk", [QK], F32, kind="ExternalInput").ap()
    d_qkw = nc.dram_tensor("qk_w", [4, QK], F32, kind="ExternalInput").ap()
    d_qkb = nc.dram_tensor("qk_b", [4, QK], F32, kind="ExternalInput").ap()
    d_rel = nc.dram_tensor("rel_table", [NBUCK], F32, kind="ExternalInput").ap()
    d_Wout = nc.dram_tensor("Wout", [HID, DIM], F32, kind="ExternalInput").ap()
    d_bout = nc.dram_tensor("bout", [DIM], F32, kind="ExternalInput").ap()
    d_Wg = nc.dram_tensor("Wg", [DIM, 8], F32, kind="ExternalInput").ap()
    d_Ws1 = nc.dram_tensor("Ws1", [DIM, HID], F32, kind="ExternalInput").ap()
    d_Ws3 = nc.dram_tensor("Ws3", [DIM, HID], F32, kind="ExternalInput").ap()
    d_Ws2 = nc.dram_tensor("Ws2", [HID, DIM], F32, kind="ExternalInput").ap()
    d_We1 = nc.dram_tensor("We1", [8, DIM, HID], F32, kind="ExternalInput").ap()
    d_We3 = nc.dram_tensor("We3", [8, DIM, HID], F32, kind="ExternalInput").ap()
    d_We2 = nc.dram_tensor("We2", [8, HID, DIM], F32, kind="ExternalInput").ap()
    d_oh = nc.dram_tensor("ohm", [NBUCK, GRP], F32, kind="ExternalInput").ap()
    d_sm = nc.dram_tensor("smask", [1, 8], F32, kind="ExternalInput").ap()
    d_out = nc.dram_tensor("out", [TOK, DIM], F32, kind="ExternalOutput").ap()

    d_text = nc.dram_tensor("t_ext", [512], F32).ap()
    d_xpx = nc.dram_tensor("xpx_spill", [NTC, 128, DIM], F32).ap()
    d_vsp = nc.dram_tensor("v_spill", [NTC, 128, HID], BF16).ap()
    d_nsp = nc.dram_tensor("n_spill", [NGRP, 128, KD * GRP], BF16).ap()
    d_ccin = nc.dram_tensor("cc_in", [QK, HID], F32).ap()
    d_ccout = nc.dram_tensor("cc_out", [NCORES * QK, HID], F32).ap()

    d_dbg = {}
    for nm in n_debug:
        shp, dt = DEBUG_TAPS[nm]
        d_dbg[nm] = nc.dram_tensor("dbg_" + nm, list(shp), dt, kind="ExternalOutput").ap()

    M = _sbuf_map()

    def sb(name, shape, dtype, off=None):
        off = M[name] if off is None else off
        return nc.alloc_sbuf_tensor_at(name, list(shape), dtype, offset=off).ap()

    lnw = sb("lnw", [128, 8], F32); lnb = sb("lnb", [128, 8], F32)
    bqk = sb("bqk", [128, 1], F32)
    qkw = sb("qkw", [128, 4], F32); qkb = sb("qkb", [128, 4], F32)
    qkw3s = sb("qkw3s", [128, 1], F32); qkb3s = sb("qkb3s", [128, 1], F32)
    bhg = sb("bhg", [128, 16], F32)
    wgf = sb("wgf", [128, 64], F32)
    wghF = sb("wghF", [128, 64], BF16)
    wglF = sb("wglF", [128, 64], BF16)
    epsc = sb("eps", [128, 1], F32)
    maskb = sb("maskb", [128, 8], F32)
    relf = sb("relf", [32, 1], F32)
    relhi = sb("relhi", [32, 1], BF16); rello = sb("rello", [32, 1], BF16)
    BT = sb("BT", [128, 512], F32)
    logits = sb("logits", [128, 128], F32)
    wtm = sb("wtm", [128, 144], F32)
    stats = sb("stats", [128, 96], F32)
    wqk = sb("wqk", [128, 1024], BF16)
    xnlo = sb("xnlo", [128, 2048], BF16)

    R = M["rings"]
    ohb = sb("ohb", [32, 256], BF16, off=R)
    trow = sb("trow", [1, 256], F32, off=R + 512)
    negrow = sb("negrow", [1, 256], F32, off=R + 1536)
    qks = sb("qks", [128, 256], F32, off=R)
    lkf = sb("lkf", [128, 256], BF16, off=R + 1024)
    linktm = sb("linktm", [128, 256], BF16, off=R + 1536)
    quadq = sb("quadq", [128, 256], BF16, off=R + 2048)
    quadk = sb("quadk", [128, 256], BF16, off=R + 2560)
    linq = sb("linq", [128, 256], BF16, off=R + 3072)
    sA = sb("sA", [128, 512], F32, off=R)
    pf = sb("pf", [128, 512], BF16, off=R + 2048)

    xnF = sb("xnF", [128, NTC * 1024], BF16)
    ACC = sb("ACC", [128, NTC * 1024], F32)
    normF = sb("normF", [128, KD * TOK], BF16)

    BIG = M["BIG"]
    xb = sb("xb", [128, 2048], F32, off=BIG)
    xcb = sb("xcb", [128, 2048], F32, off=BIG + 8192)
    sqb = sb("sqb", [128, 1024], F32, off=BIG + 16384)
    xnb = sb("xnb", [128, 2048], F32, off=BIG + 20480)
    xpxb = sb("xpxb", [128, 2048], F32, off=BIG + 28672)
    nrawb = sb("nrawb", [128, 2048], BF16, off=BIG + 36864)
    xnbf = sb("xnbf", [128, 2048], BF16, off=BIG + 40960)
    xnlob = sb("xnlob", [128, 2048], BF16, off=BIG + 45056)
    boutbc = sb("boutbc", [128, 1024], F32, off=BIG + 49152)

    wset = [
        {"w1": sb("w1_0", [128, 8192], BF16, off=BIG),
         "w3": sb("w3_0", [128, 8192], BF16, off=BIG + 16384),
         "w2": sb("w2_0", [128, 8192], BF16, off=BIG + 32768)},
        {"w1": sb("w1_1", [128, 8192], BF16, off=BIG + 49152),
         "w3": sb("w3_1", [128, 8192], BF16, off=M["normF"]),
         "w2": sb("w2_1", [128, 8192], BF16, off=M["normF"] + 16384)},
    ]
    gate_r = sb("gate_r", [128, 512], F32, off=BIG + 49152)
    outh_r = sb("outh_r", [128, 512], BF16, off=BIG + 51200)
    attnT = sb("attnT", [128, 512], BF16, off=BIG + 52224)
    simy = sb("simy", [128, 512], F32, off=BIG + 53248)
    fill = sb("fill", [128, 256], F32, off=BIG + 55296)
    xnlo_b = sb("xnlo_b", [128, 2048], BF16, off=BIG + 56320)

    AC = M["ACC"]
    whv = sb("whv", [128, KD * HID], BF16, off=AC)
    bhvbc = sb("bhvbc", [128, HID], F32, off=AC + 32768)
    vbufA = sb("vbufA", [128, 2 * HID], BF16, off=AC + 40960)
    kvaccA = sb("kvaccA", [128, HID], F32, off=AC + 49152)
    vtmp = sb("vtmp", [128, 1024], F32, off=AC + 57344)

    wout_sb = sb("wout_sb", [128, NHC * 1024], BF16, off=BIG)
    vring = sb("vring", [128, 4 * HID], BF16, off=BIG + 32768)
    whg_sb = sb("whg_sb", [128, KD * HID], BF16, off=M["xnF"])
    normg = sb("normg", [128, 2 * KD * GRP], BF16, off=M["normF"])
    kvrun = sb("kvrun", [128, HID], F32, off=M["normF"] + 8192)
    kvbf = sb("kvbf", [128, HID], BF16, off=M["normF"] + 16384)
    ccr = sb("ccr", [128, HID], F32, off=M["normF"] + 20480)

    PS = [nc.alloc_psum_tensor(f"ps{i}", [128, 512], F32).ap() for i in range(8)]

    es = contextlib.ExitStack()
    sem_names = ["ld", "sp", "sc", "pe", "act", "dve", "cc", "out"]
    handles = {nm: es.enter_context(nc.semaphore("s_" + nm)) for nm in sem_names}
    block = es.enter_context(nc.Block())

    def program(ctx):
        S = {nm: Sem(handles[nm]) for nm in sem_names}
        ld, sp, sc, pe, act, dve, cc, out_s = (
            S["ld"], S["sp"], S["sc"], S["pe"], S["act"], S["dve"], S["cc"], S["out"])

        def dbg_dump(name, src_ap):
            if name in d_dbg:
                ctx.op("gpsimd", lambda g: g.dma_start(out=d_dbg[name], in_=src_ap), sp, 16)

        ctx.op("sync", lambda e: e.dma_start(out=relf, in_=d_rel[:, None]), ld, 16)
        ctx.op("sync", lambda e: e.dma_start(out=bqk, in_=d_bqk[:, None]), ld, 16)
        for h in range(4):
            ctx.op("sync", lambda e, h=h: e.dma_start(out=qkw[:, h:h + 1], in_=d_qkw[h, :][:, None]), ld, 16)
            ctx.op("sync", lambda e, h=h: e.dma_start(out=qkb[:, h:h + 1], in_=d_qkb[h, :][:, None]), ld, 16)
        for c in range(8):
            ctx.op("sync", lambda e, c=c: e.dma_start(out=lnw[:, c:c + 1], in_=d_lnw[c * 128:(c + 1) * 128][:, None]), ld, 16)
            ctx.op("sync", lambda e, c=c: e.dma_start(out=lnb[:, c:c + 1], in_=d_lnb[c * 128:(c + 1) * 128][:, None]), ld, 16)
        for c in range(16):
            ctx.op("sync", lambda e, c=c: e.dma_start(out=bhg[:, c:c + 1], in_=d_bh[HID + c * 128: HID + (c + 1) * 128][:, None]), ld, 16)
        ctx.op("sync", lambda e: e.dma_start(
            out=wgf.rearrange("p (c n) -> p c n", n=8),
            in_=d_Wg.rearrange("(c p) n -> p c n", p=128)), ld, 16)
        ld_pro = ld.n

        ctx.op("gpsimd", lambda g: g.dma_start(out=ohb, in_=d_oh), sp, 16)
        ctx.op("gpsimd", lambda g: g.dma_start(out=maskb, in_=d_sm.broadcast_to([128, 8])), sp, 16)
        ctx.op("gpsimd", lambda g: g.dma_start(out=boutbc, in_=d_bout[None, :].broadcast_to([128, DIM])), sp, 16)
        ctx.op("gpsimd", lambda g: g.dma_start(out=bhvbc, in_=d_bh[None, 0:HID].broadcast_to([128, HID])), sp, 16)
        sp_bc = sp.n
        ctx.op("gpsimd", lambda g: g.dma_start(
            out=wqk.rearrange("p (c n) -> p c n", n=128),
            in_=d_Wqk.rearrange("(c p) n -> p c n", p=128)), sp, 16)
        ctx.op("gpsimd", lambda g: g.dma_start(
            out=whv.rearrange("p (c n) -> p c n", n=HID),
            in_=d_Wh.rearrange("(c p) n -> p c n", p=128)[:, :, 0:HID]), sp, 16)
        sp_wts = sp.n

        ctx.wait("vector", ld, ld_pro)
        ctx.wait("vector", sp, sp_bc)
        ctx.op("vector", lambda v: v.memset(epsc, EPS))
        ctx.op("vector", lambda v: v.memset(negrow, NEG))
        ctx.op("vector", lambda v: v.memset(kvaccA, 0.0))
        ctx.op("vector", lambda v: v.tensor_scalar(out=qkw3s, in0=qkw[:, 3:4], scalar1=1.0 / GRP, scalar2=None, op0=OP.mult))
        ctx.op("vector", lambda v: v.tensor_scalar(out=qkb3s, in0=qkb[:, 3:4], scalar1=1.0 / GRP, scalar2=None, op0=OP.mult))
        ctx.op("vector", lambda v: v.tensor_scalar(out=relf, in0=relf, scalar1=BSCALE, scalar2=None, op0=OP.mult))
        ctx.op("vector", lambda v: v.tensor_copy(out=wghF, in_=wgf))
        ctx.op("vector", lambda v: v.tensor_copy(out=relhi, in_=relf))
        ctx.op("vector", lambda v: v.tensor_sub(out=wglF, in0=wgf, in1=wghF))
        ctx.op("vector", lambda v: v.tensor_sub(out=rello, in0=relf, in1=relhi))
        ctx.op("vector", lambda v: v.tensor_copy(out=fill[:, 0:128], in_=fill[:, 128:256]), dve, 1)
        dve_pro = dve.n

        ctx.wait("tensor", dve, dve_pro)
        ctx.op("tensor", lambda t: t.matmul(out=PS[7][0:1, 0:256], lhsT=relhi, rhs=ohb, start=True, stop=False), pe, 1)
        ctx.op("tensor", lambda t: t.matmul(out=PS[7][0:1, 0:256], lhsT=rello, rhs=ohb, start=False, stop=True), pe, 1)
        pe_bt = pe.n
        ctx.wait("vector", pe, pe_bt)
        ctx.op("vector", lambda v: v.tensor_copy(out=trow, in_=PS[7][0:1, 0:256]))
        ctx.op("vector", lambda v: v.tensor_copy(out=fill[:, 0:128], in_=fill[:, 128:256]), dve, 1)
        dve_bt = dve.n
        ctx.wait("gpsimd", dve, dve_bt)
        ctx.op("gpsimd", lambda g: g.dma_start(out=d_text[None, 0:256], in_=negrow), sp, 16)
        ctx.op("gpsimd", lambda g: g.dma_start(out=d_text[None, 256:512], in_=trow), sp, 16)
        sp_text = sp.n

        for tc0 in range(2):
            ctx.op("sync", lambda e, tc=tc0: e.dma_start(
                out=xb[:, (tc % 2) * 1024:(tc % 2) * 1024 + 1024],
                in_=d_x[tc * 128:(tc + 1) * 128, :]), ld, 16)
        ld_x = [ld_pro + 16, ld_pro + 32]

        act_sqrt2 = []
        dve_xpx_l = []
        dve_ln = []
        dve_nsc = []
        dve_rtr_evac = []
        pe_rtr = []
        sc_tr = []
        sp_xpx = []
        act_vev = []
        sp_vsp = []
        dve_kv = []
        dve_vt = []
        state = {"dve_lkf_last": dve_pro, "sc_lkt": None, "pe_kv_last": None,
                 "act_qk_last": None}

        def spc(v):
            return v.tensor_copy(out=fill[:, 0:128], in_=fill[:, 128:256])

        def xnlo_slot(tc):
            s4 = tc % 4
            base = xnlo if s4 < 2 else xnlo_b
            return base[:, (s4 % 2) * 1024:(s4 % 2) * 1024 + 1024]

        def emit_ln(tc):
            X = xb[:, (tc % 2) * 1024:(tc % 2) * 1024 + 1024]
            XN = xnb[:, (tc % 2) * 1024:(tc % 2) * 1024 + 1024]
            XP = xpxb[:, (tc % 2) * 1024:(tc % 2) * 1024 + 1024]
            NR = nrawb[:, (tc % 2) * 1024:(tc % 2) * 1024 + 1024]
            XBF = xnbf[:, (tc % 2) * 1024:(tc % 2) * 1024 + 1024]
            XLO = xnlob[:, (tc % 2) * 1024:(tc % 2) * 1024 + 1024]
            st = stats[:, (tc % 2) * 16:(tc % 2) * 16 + 16]

            def ln_stats(SRC, o):
                ctx.op("vector", lambda v: v.tensor_mul(out=sqb[:, 0:1024], in0=SRC, in1=SRC))
                ctx.op("vector", lambda v: v.reduce_sum(out=st[:, o:o + 1], in_=SRC, axis=AX))
                ctx.op("vector", lambda v: v.reduce_sum(out=st[:, o + 1:o + 2], in_=sqb[:, 0:1024], axis=AX))
                ctx.op("vector", lambda v: v.tensor_scalar(out=st[:, o:o + 1], in0=st[:, o:o + 1], scalar1=1.0 / DIM, scalar2=None, op0=OP.mult))
                ctx.op("vector", lambda v: v.tensor_scalar(out=st[:, o + 1:o + 2], in0=st[:, o + 1:o + 2], scalar1=1.0 / DIM, scalar2=None, op0=OP.mult))
                ctx.op("vector", lambda v: v.tensor_mul(out=st[:, o + 2:o + 3], in0=st[:, o:o + 1], in1=st[:, o:o + 1]))
                ctx.op("vector", spc)
                ctx.op("vector", lambda v: v.tensor_sub(out=st[:, o + 3:o + 4], in0=st[:, o + 1:o + 2], in1=st[:, o + 2:o + 3]))
                ctx.op("vector", spc, dve, 1)
                ctx.wait("scalar", dve, dve.n)
                ctx.op("scalar", lambda s: s.nop(cycle_cnt=800))
                ctx.op("scalar", lambda s: s.activation(out=st[:, o + 4:o + 5], in_=st[:, o + 3:o + 4], func=AF.Sqrt, bias=epsc[:, 0:1]), act, 1)
                ctx.wait("vector", act, act.n)
                ctx.op("vector", lambda v: v.nop(cycle_cnt=800))
                ctx.op("vector", lambda v: v.reciprocal(out=st[:, o + 5:o + 6], in_=st[:, o + 4:o + 5]))
                ctx.op("vector", spc)

            ctx.wait("vector", ld, ld_x[tc])
            if tc >= 2:
                ctx.wait("vector", sc, sc_tr[tc - 2])
                ctx.wait("vector", act, act_sqrt2[tc - 2])
                ctx.wait("vector", sp, sp_xpx[tc - 2])
            ln_stats(X, 0)
            ctx.op("vector", lambda v: v.tensor_scalar(out=XN, in0=X, scalar1=st[:, 0:1], scalar2=st[:, 5:6], op0=OP.subtract, op1=OP.mult))
            ctx.op("vector", lambda v: v.tensor_add(out=XP, in0=X, in1=XN))
            ctx.op("vector", lambda v: v.tensor_add(out=XP, in0=XP, in1=boutbc), dve, 1)
            dve_xpx_l.append(dve.n)
            ln_stats(XN, 8)
            act_sqrt2.append(act.n)
            ctx.op("vector", lambda v: v.tensor_scalar(out=NR, in0=XN, scalar1=st[:, 8:9], scalar2=st[:, 13:14], op0=OP.subtract, op1=OP.mult))
            ctx.op("vector", lambda v: v.tensor_copy(out=XBF, in_=XN))
            ctx.op("vector", lambda v: v.tensor_sub(out=XLO, in0=XN, in1=XBF), dve, 1)
            dve_ln.append(dve.n)

            ctx.wait("gpsimd", dve, dve_xpx_l[-1])
            ctx.op("gpsimd", lambda g_, tc=tc, XP=XP: g_.dma_start(out=d_xpx[tc], in_=XP), sp, 16)
            sp_xpx.append(sp.n)

            if tc + 2 < NTC:
                ctx.wait("sync", dve, dve_xpx_l[-1])
                ctx.op("sync", lambda e, tc=tc: e.dma_start(
                    out=xb[:, (tc % 2) * 1024:(tc % 2) * 1024 + 1024],
                    in_=d_x[(tc + 2) * 128:(tc + 3) * 128, :]), ld, 16)
                ld_x.append(ld.n)

            ctx.wait("sync", dve, dve_ln[-1])
            ctx.wait("scalar", dve, dve_ln[-1])
            if tc >= 4:
                ctx.wait("scalar", pe, pe_rtr[tc - 4])
            XLOD = xnlo_slot(tc)
            for c in range(8):
                ctx.op("sync", lambda e, c=c, tc=tc, NR=NR: e.dma_start(
                    out=normF[:, c * TOK + tc * 128: c * TOK + tc * 128 + 128],
                    in_=NR[:, c * 128:c * 128 + 128], transpose=True), sc, 16)
            sc_nrm = sc.n
            for c in range(8):
                q = "sync" if c % 2 == 0 else "scalar"
                ctx.op(q, lambda e, c=c, tc=tc, XBF=XBF: e.dma_start(
                    out=xnF[:, c * TOK + tc * 128: c * TOK + tc * 128 + 128],
                    in_=XBF[:, c * 128:c * 128 + 128], transpose=True), sc, 16)
                ctx.op("scalar", lambda e, c=c, tc=tc, XLO=XLO, XLOD=XLOD: e.dma_start(
                    out=XLOD[:, c * 128:c * 128 + 128],
                    in_=XLO[:, c * 128:c * 128 + 128], transpose=True), sc, 16)
            sc_tr.append(sc.n)

            ctx.wait("vector", sc, sc_tr[-1])
            for c in range(8):
                ctx.op("vector", lambda v, c=c, tc=tc: v.tensor_scalar(
                    out=normF[:, c * TOK + tc * 128: c * TOK + tc * 128 + 128],
                    in0=normF[:, c * TOK + tc * 128: c * TOK + tc * 128 + 128],
                    scalar1=lnw[:, c:c + 1], scalar2=lnb[:, c:c + 1],
                    op0=OP.mult, op1=OP.add), dve, 1)
            dve_nsc.append(dve.n)

        def emit_router(tc):
            ctx.wait("tensor", sc, sc_tr[tc])
            ctx.wait("tensor", dve, dve_pro)
            if tc >= 1:
                ctx.wait("tensor", dve, dve_rtr_evac[tc - 1])
            if dve_kv:
                ctx.wait("tensor", dve, dve_kv[-1])
            for c in range(8):
                ctx.op("tensor", lambda t, c=c, tc=tc: t.matmul(
                    out=PS[6][:, 0:8], lhsT=xnF[:, c * TOK + tc * 128: c * TOK + tc * 128 + 128],
                    rhs=wghF[:, c * 8:c * 8 + 8], start=(c == 0), stop=False), pe, 1)
            for c in range(8):
                ctx.op("tensor", lambda t, c=c, tc=tc: t.matmul(
                    out=PS[6][:, 0:8], lhsT=xnF[:, c * TOK + tc * 128: c * TOK + tc * 128 + 128],
                    rhs=wglF[:, c * 8:c * 8 + 8], start=False, stop=False), pe, 1)
            XLOD = xnlo_slot(tc)
            for c in range(8):
                ctx.op("tensor", lambda t, c=c, tc=tc, XLOD=XLOD: t.matmul(
                    out=PS[6][:, 0:8], lhsT=XLOD[:, c * 128:c * 128 + 128],
                    rhs=wghF[:, c * 8:c * 8 + 8], start=False, stop=(c == 7)), pe, 1)
            pe_rtr.append(pe.n)
            ctx.wait("vector", pe, pe.n)
            ctx.op("vector", lambda v, tc=tc: v.tensor_copy(out=logits[:, tc * 8:tc * 8 + 8], in_=PS[6][:, 0:8]), dve, 1)
            dve_rtr_evac.append(dve.n)

        def emit_attnA(g):
            ctx.wait("tensor", dve, dve_nsc[2 * g + 1])
            ctx.wait("tensor", sp, sp_wts)
            if state["act_qk_last"] is not None:
                ctx.wait("tensor", act, state["act_qk_last"])
            for c in range(KD):
                ctx.op("tensor", lambda t, c=c, g=g: t.matmul(
                    out=PS[4][:, 0:256], lhsT=wqk[:, c * 128:c * 128 + 128],
                    rhs=normF[:, c * TOK + g * 256: c * TOK + g * 256 + 256],
                    start=(c == 0), stop=(c == KD - 1)), pe, 1)
            pe_qk = pe.n
            ctx.wait("scalar", pe, pe_qk)
            if g > 0:
                ctx.wait("scalar", dve, state["dve_lkf_last"])
                ctx.wait("scalar", sc, state["sc_lkt"])
            ctx.op("scalar", lambda s: s.activation(out=qks, in_=PS[4][:, 0:256], func=AF.Silu, bias=bqk[:, 0:1]), act, 1)
            act_qk = act.n
            state["act_qk_last"] = act_qk
            ctx.wait("vector", act, act_qk)
            if g > 0:
                ctx.wait("vector", sc, state["sc_lkt"])
            ctx.op("vector", lambda v: v.tensor_scalar(out=lkf, in0=qks, scalar1=qkw3s[:, 0:1], scalar2=qkb3s[:, 0:1], op0=OP.mult, op1=OP.add), dve, 1)
            dve_lkf = dve.n
            state["dve_lkf_last"] = dve_lkf
            ctx.wait("sync", dve, dve_lkf)
            if g > 0:
                ctx.wait("sync", pe, state["pe_kv_last"])
            for i in range(2):
                ctx.op("sync", lambda e, i=i: e.dma_start(
                    out=linktm[:, i * 128:i * 128 + 128],
                    in_=lkf[:, i * 128:i * 128 + 128], transpose=True), sc, 16)
            state["sc_lkt"] = sc.n

            for i in range(2):
                tc = 2 * g + i
                vb = vbufA[:, (tc % 2) * HID:(tc % 2) * HID + HID]
                for hb in range(4):
                    b = (i * 4 + hb) % 4
                    if len(dve_vt) >= 4:
                        ctx.wait("tensor", dve, dve_vt[-4])
                    if len(sp_vsp) >= 2 and hb == 0:
                        ctx.wait("tensor", sp, sp_vsp[-2])
                    for c in range(KD):
                        ctx.op("tensor", lambda t, c=c, tc=tc, hb=hb, b=b: t.matmul(
                            out=PS[b][:, 0:512],
                            lhsT=normF[:, c * TOK + tc * 128: c * TOK + tc * 128 + 128],
                            rhs=whv[:, c * HID + hb * 512: c * HID + hb * 512 + 512],
                            start=(c == 0), stop=(c == KD - 1)), pe, 1)
                    pe_v = pe.n
                    ctx.wait("vector", pe, pe_v)
                    if len(act_vev) >= 2:
                        ctx.wait("vector", act, act_vev[-2])
                    ctx.op("vector", lambda v, hb=hb, b=b: v.tensor_add(
                        out=vtmp[:, (hb % 2) * 512:(hb % 2) * 512 + 512],
                        in0=PS[b][:, 0:512], in1=bhvbc[:, hb * 512:hb * 512 + 512]), dve, 1)
                    dve_vt.append(dve.n)
                    ctx.wait("scalar", dve, dve_vt[-1])
                    ctx.op("scalar", lambda s, hb=hb, vb=vb: s.activation(
                        out=vb[:, hb * 512:hb * 512 + 512],
                        in_=vtmp[:, (hb % 2) * 512:(hb % 2) * 512 + 512], func=AF.Silu), act, 1)
                    act_vev.append(act.n)
                ctx.wait("gpsimd", act, act_vev[-1])
                ctx.op("gpsimd", lambda gp, tc=tc, vb=vb: gp.dma_start(out=d_vsp[tc], in_=vb), sp, 16)
                sp_vsp.append(sp.n)

            ctx.wait("tensor", sc, state["sc_lkt"])
            ctx.wait("tensor", act, act_vev[-1])
            if dve_kv:
                ctx.wait("tensor", dve, dve_kv[-1])
            if dve_rtr_evac:
                ctx.wait("tensor", dve, dve_rtr_evac[-1])
            kv_ev = []
            for hb in range(4):
                b = 5 + (hb % 2)
                if hb >= 2:
                    ctx.wait("tensor", dve, kv_ev[hb - 2])
                for i in range(2):
                    vb = vbufA[:, ((2 * g + i) % 2) * HID: ((2 * g + i) % 2) * HID + HID]
                    ctx.op("tensor", lambda t, i=i, hb=hb, b=b, vb=vb: t.matmul(
                        out=PS[b][:, 0:512], lhsT=linktm[:, i * 128:i * 128 + 128],
                        rhs=vb[:, hb * 512:hb * 512 + 512],
                        start=(i == 0), stop=(i == 1)), pe, 1)
                pe_kv = pe.n
                ctx.wait("vector", pe, pe_kv)
                ctx.op("vector", lambda v, hb=hb, b=b: v.tensor_add(
                    out=kvaccA[:, hb * 512:hb * 512 + 512], in0=PS[b][:, 0:512],
                    in1=kvaccA[:, hb * 512:hb * 512 + 512]), dve, 1)
                kv_ev.append(dve.n)
            dve_kv.append(dve.n)
            state["pe_kv_last"] = pe.n

        def emit_set_load(s):
            e, h = s // 2, s % 2
            slot = wset[s % 2]
            if e == 0:
                w1src = d_Ws1.rearrange("(c p) n -> p c n", p=128)[:, :, h * 1024:(h + 1) * 1024]
                w3src = d_Ws3.rearrange("(c p) n -> p c n", p=128)[:, :, h * 1024:(h + 1) * 1024]
                w2src = d_Ws2[h * 1024:(h + 1) * 1024, :].rearrange("(f p) n -> p f n", p=128)
            else:
                w1src = d_We1[e - 1].rearrange("(c p) n -> p c n", p=128)[:, :, h * 1024:(h + 1) * 1024]
                w3src = d_We3[e - 1].rearrange("(c p) n -> p c n", p=128)[:, :, h * 1024:(h + 1) * 1024]
                w2src = d_We2[e - 1][h * 1024:(h + 1) * 1024, :].rearrange("(f p) n -> p f n", p=128)
            ctx.op("gpsimd", lambda g: g.dma_start(
                out=slot["w1"].rearrange("p (c n) -> p c n", n=1024), in_=w1src), sp, 16)
            ctx.op("gpsimd", lambda g: g.dma_start(
                out=slot["w3"].rearrange("p (c n) -> p c n", n=1024), in_=w3src), sp, 16)
            ctx.op("gpsimd", lambda g: g.dma_start(
                out=slot["w2"].rearrange("p (f n) -> p f n", n=1024), in_=w2src), sp, 16)
            return sp.n

        sp_sets = []
        sp_nsp_l = []
        for g in range(NGRP):
            emit_ln(2 * g)
            emit_ln(2 * g + 1)
            ctx.wait("gpsimd", dve, dve_nsc[-1])
            for c in range(KD):
                ctx.op("gpsimd", lambda gp, g=g, c=c: gp.dma_start(
                    out=d_nsp[g][:, c * 256:c * 256 + 256],
                    in_=normF[:, c * TOK + g * 256:c * TOK + g * 256 + 256]), sp, 16)
            sp_nsp_l.append(sp.n)
            if g == NGRP - 1:
                ctx.wait("gpsimd", sc, sc_tr[-1])
                sp_sets.append(emit_set_load(0))
            if g >= 1:
                emit_router(2 * g - 2)
                emit_router(2 * g - 1)
                emit_attnA(g - 1)
        emit_router(2 * NGRP - 2)
        emit_router(2 * NGRP - 1)
        emit_attnA(NGRP - 1)
        pe_a_end = pe.n
        dve_ln_done = dve.n
        sc_ln_done = sc.n
        ctx.wait("gpsimd", dve, dve_kv[-1])
        dbg_dump("kvacc", kvaccA)
        dbg_dump("xnF", xnF)
        dbg_dump("normF", normF)

        ctx.wait("sync", sp, sp_text)
        for K in range(256):
            kc, k = K // 128, K % 128
            src = AP(tensor=d_text.tensor, offset=256 - K, ap=[[1, 1], [1, 256]])
            ctx.op("sync", lambda e, kc=kc, k=k, src=src: e.dma_start(
                out=BT[k:k + 1, kc * 256:(kc + 1) * 256], in_=src), sc, 16)
        sc_bt = sc.n
        ctx.wait("gpsimd", sc, sc_bt)
        dbg_dump("BT", BT)

        ctx.wait("scalar", dve, dve_rtr_evac[-1])
        ctx.op("scalar", lambda s: s.activation(out=logits, in_=logits, func=AF.Sigmoid), act, 1)
        act_sig = act.n
        ctx.wait("vector", act, act_sig)
        for pt in range(NTC // 2):
            els = []
            for j in range(2):
                tc = 2 * pt + j
                base = 32 + 32 * j
                els.append(dict(
                    g8=logits[:, tc * 8:tc * 8 + 8],
                    m1=stats[:, base:base + 1], m2=stats[:, base + 1:base + 2],
                    s12=stats[:, base + 2:base + 3],
                    E=stats[:, base + 4:base + 12], G2=stats[:, base + 12:base + 20],
                    w=wtm[:, tc * 9 + 1: tc * 9 + 9]))
            for E_ in els:
                ctx.op("vector", lambda v, e=E_: v.reduce_max(out=e["m1"], in_=e["g8"], axis=AX))
            for E_ in els:
                ctx.op("vector", lambda v, e=E_: v.tensor_scalar(out=e["E"], in0=e["g8"], scalar1=e["m1"], scalar2=None, op0=OP.is_equal))
            for E_ in els:
                ctx.op("vector", lambda v, e=E_: v.scalar_tensor_tensor(out=e["G2"], in0=e["E"], scalar=NEG, in1=e["g8"], op0=OP.mult, op1=OP.add))
            for E_ in els:
                ctx.op("vector", lambda v, e=E_: v.reduce_max(out=e["m2"], in_=e["G2"], axis=AX))
            for E_ in els:
                ctx.op("vector", lambda v, e=E_: v.tensor_scalar(out=e["E"], in0=e["g8"], scalar1=e["m2"], scalar2=None, op0=OP.is_ge))
            for E_ in els:
                ctx.op("vector", lambda v, e=E_: v.tensor_add(out=e["s12"], in0=e["m1"], in1=e["m2"]))
            for E_ in els:
                ctx.op("vector", lambda v, e=E_: v.reciprocal(out=e["s12"], in_=e["s12"]))
            for E_ in els:
                ctx.op("vector", lambda v, e=E_: v.tensor_scalar(out=e["s12"], in0=e["s12"], scalar1=1.0 / 3.0, scalar2=None, op0=OP.mult))
            for E_ in els:
                ctx.op("vector", lambda v, e=E_: v.tensor_mul(out=e["E"], in0=e["E"], in1=e["g8"]))
            for E_ in els:
                ctx.op("vector", lambda v, e=E_: v.tensor_scalar(
                    out=e["w"], in0=e["E"], scalar1=e["s12"], scalar2=None, op0=OP.mult), dve, 1)
        dve_router = dve.n
        ctx.wait("gpsimd", dve, dve_router)
        dbg_dump("wtm", wtm)
        dbg_dump("logits", logits)

        ctx.wait("gpsimd", dve, dve_kv[-1])
        ctx.op("gpsimd", lambda g: g.dma_start(out=d_ccin, in_=kvaccA), sp, 16)
        sp_ccin = sp.n
        ctx.wait("gpsimd", sp, sp_ccin)
        ctx.op("gpsimd", lambda g: g.collective_compute(
            "AllGather", OP.bypass, replica_groups=[list(range(NCORES))],
            ins=[d_ccin.opt()], outs=[d_ccout.opt()]), cc, 1)

        sp_nsp = sp_nsp_l[-1]

        ctx.wait("gpsimd", pe, pe_a_end)
        for tc in range(NTC):
            ctx.op("gpsimd", lambda g, tc=tc: g.dma_start(
                out=ACC[:, tc * 1024:(tc + 1) * 1024], in_=d_xpx[tc]), sp, 16)
        sp_accinit = sp.n

        pe_sets = []
        dve_yev = []
        sp_wout = None
        sp_whg = None

        ctx.wait("gpsimd", sp, sp_nsp)
        ctx.wait("gpsimd", pe, pe_rtr[-1])
        sp_sets.append(emit_set_load(1))

        ctx.wait("vector", sp, sp_accinit)

        for s in range(18):
            e = s // 2
            slot = wset[s % 2]
            ctx.wait("tensor", sp, sp_sets[s])
            if s == 0:
                ctx.wait("tensor", dve, dve_kv[-1])
                ctx.wait("tensor", dve, dve_router)
            for strip in range(NGRP):
                pf_l = []
                silu_l = []

                def emit_y(f):
                    ctx.wait("tensor", dve, pf_l[f])
                    for tcl in range(2):
                        for oh2 in range(2):
                            ctx.op("tensor", lambda t, f=f, tcl=tcl, oh2=oh2: t.matmul(
                                out=PS[2 * tcl + oh2][:, 0:512],
                                lhsT=pf[:, (f % 2) * 256 + tcl * 128:(f % 2) * 256 + tcl * 128 + 128],
                                rhs=slot["w2"][:, f * 1024 + oh2 * 512: f * 1024 + oh2 * 512 + 512],
                                start=(f == 0), stop=(f == 7)), pe, 1)

                for f in range(8):
                    ab = 4 + 2 * (f % 2)
                    if f >= 2:
                        ctx.wait("tensor", act, silu_l[f - 2])
                        ctx.wait("tensor", dve, pf_l[f - 2])
                    for c in range(KD):
                        ctx.op("tensor", lambda t, c=c, f=f, ab=ab, strip=strip: t.matmul(
                            out=PS[ab][:, 0:256],
                            lhsT=slot["w1"][:, c * 1024 + f * 128: c * 1024 + f * 128 + 128],
                            rhs=xnF[:, c * TOK + strip * 256: c * TOK + strip * 256 + 256],
                            start=(c == 0), stop=(c == KD - 1)), pe, 1)
                    for c in range(KD):
                        ctx.op("tensor", lambda t, c=c, f=f, ab=ab, strip=strip: t.matmul(
                            out=PS[ab + 1][:, 0:256],
                            lhsT=slot["w3"][:, c * 1024 + f * 128: c * 1024 + f * 128 + 128],
                            rhs=xnF[:, c * TOK + strip * 256: c * TOK + strip * 256 + 256],
                            start=(c == 0), stop=(c == KD - 1)), pe, 1)
                    pe_ab = pe.n
                    ctx.wait("scalar", pe, pe_ab)
                    ctx.op("scalar", lambda sE, f=f, ab=ab: sE.activation(
                        out=sA[:, (f % 2) * 256:(f % 2) * 256 + 256], in_=PS[ab][:, 0:256], func=AF.Silu), act, 1)
                    silu_l.append(act.n)
                    ctx.wait("vector", act, act.n)
                    ctx.op("vector", lambda v, f=f, ab=ab: v.tensor_mul(
                        out=pf[:, (f % 2) * 256:(f % 2) * 256 + 256],
                        in0=sA[:, (f % 2) * 256:(f % 2) * 256 + 256], in1=PS[ab + 1][:, 0:256]), dve, 1)
                    pf_l.append(dve.n)
                    if f == 0 and dve_yev:
                        ctx.wait("tensor", dve, dve_yev[-1])
                    if f >= 1:
                        emit_y(f - 1)
                emit_y(7)
                pe_y = pe.n
                ctx.wait("vector", pe, pe_y)
                for tcl in range(2):
                    tcg = strip * 2 + tcl
                    for oh2 in range(2):
                        accsl = ACC[:, tcg * 1024 + oh2 * 512: tcg * 1024 + oh2 * 512 + 512]
                        if e == 0:
                            ctx.op("vector", lambda v, tcl=tcl, oh2=oh2, accsl=accsl: v.scalar_tensor_tensor(
                                out=accsl, in0=PS[2 * tcl + oh2][:, 0:512],
                                scalar=1.0 / 3.0, in1=accsl, op0=OP.mult, op1=OP.add), dve, 1)
                        else:
                            wcol = wtm[:, tcg * 9 + e: tcg * 9 + e + 1]
                            ctx.op("vector", lambda v, tcl=tcl, oh2=oh2, accsl=accsl, wcol=wcol: v.scalar_tensor_tensor(
                                out=accsl, in0=PS[2 * tcl + oh2][:, 0:512],
                                scalar=wcol, in1=accsl, op0=OP.mult, op1=OP.add), dve, 1)
                dve_yev.append(dve.n)
            pe_sets.append(pe.n)
            if s + 2 < 18:
                ctx.wait("gpsimd", pe, pe_sets[-1])
                sp_sets.append(emit_set_load(s + 2))
            elif s == 16:
                ctx.wait("gpsimd", pe, pe_sets[-1])
                ctx.op("gpsimd", lambda g: g.dma_start(
                    out=wout_sb.rearrange("p (c n) -> p c n", n=1024),
                    in_=d_Wout.rearrange("(c p) n -> p c n", p=128)), sp, 16)
                sp_wout = sp.n
            elif s == 17:
                ctx.wait("gpsimd", pe, pe_sets[-1])
                ctx.op("gpsimd", lambda g: g.dma_start(
                    out=whg_sb.rearrange("p (c n) -> p c n", n=HID),
                    in_=d_Wh.rearrange("(c p) n -> p c n", p=128)[:, :, HID:2 * HID]), sp, 16)
                sp_whg = sp.n
        ctx.wait("gpsimd", dve, dve_yev[-1])
        dbg_dump("ACC_ffn", ACC)

        ctx.wait("gpsimd", cc, 1)
        for r in range(NCORES):
            ctx.op("gpsimd", lambda g, r=r: g.dma_start(out=ccr, in_=d_ccout[r * 128:(r + 1) * 128, :]), sp, 16)
            sp_ccr = sp.n
            ctx.wait("vector", sp, sp_ccr)
            if r == 0:
                ctx.op("vector", lambda v: v.tensor_scalar(
                    out=kvrun, in0=ccr, scalar1=maskb[:, 0:1], scalar2=None, op0=OP.mult), dve, 1)
            else:
                ctx.op("vector", lambda v, r=r: v.scalar_tensor_tensor(
                    out=kvrun, in0=ccr, scalar=maskb[:, r:r + 1], in1=kvrun,
                    op0=OP.mult, op1=OP.add), dve, 1)
            ctx.wait("gpsimd", dve, dve.n)
        dve_base = dve.n
        dbg_dump("kvbase", kvrun)

        sp_vld = []
        sp_nld = []

        def emit_group_loads(g):
            for i in range(2):
                tc = 2 * g + i
                ctx.op("gpsimd", lambda gp, g=g, tc=tc, i=i: gp.dma_start(
                    out=vring[:, ((2 * g + i) % 4) * HID: ((2 * g + i) % 4) * HID + HID],
                    in_=d_vsp[tc]), sp, 16)
                sp_vld.append(sp.n)
            ctx.op("gpsimd", lambda gp, g=g: gp.dma_start(
                out=normg[:, (g % 2) * KD * 256:(g % 2) * KD * 256 + KD * 256], in_=d_nsp[g]), sp, 16)
            sp_nld.append(sp.n)

        ctx.wait("gpsimd", sp, sp_whg)
        emit_group_loads(0)
        emit_group_loads(1)

        pe_bgrp = []
        sc_lkt2_last = None
        for g in range(NGRP):
            nrm = normg[:, (g % 2) * KD * 256:(g % 2) * KD * 256 + KD * 256]
            ctx.wait("tensor", sp, sp_nld[g])
            ctx.wait("tensor", sp, sp_vld[2 * g + 1])
            if g == 0:
                ctx.wait("tensor", sp, sp_wout)
                ctx.wait("tensor", sp, sp_whg)
                ctx.wait("tensor", sc, sc_bt)
                ctx.wait("tensor", dve, dve_yev[-1])
                ctx.wait("tensor", dve, dve_base)

            if g >= 1:
                ctx.wait("vector", pe, pe_bgrp[g - 1])
                ctx.wait("vector", sc, sc_lkt2_last)
            ctx.wait("vector", dve, dve_base)
            ctx.op("vector", lambda v: v.tensor_copy(out=kvbf, in_=kvrun), dve, 1)

            for c in range(KD):
                ctx.op("tensor", lambda t, c=c, nrm=nrm: t.matmul(
                    out=PS[4][:, 0:256], lhsT=wqk[:, c * 128:c * 128 + 128],
                    rhs=nrm[:, c * 256:c * 256 + 256],
                    start=(c == 0), stop=(c == KD - 1)), pe, 1)
            pe_qk2 = pe.n
            ctx.wait("scalar", pe, pe_qk2)
            ctx.op("scalar", lambda s: s.activation(out=qks, in_=PS[4][:, 0:256], func=AF.Silu, bias=bqk[:, 0:1]), act, 1)
            act_qk2 = act.n
            ctx.wait("vector", act, act_qk2)
            for (dst, hh) in ((quadq, 0), (linq, 1), (quadk, 2)):
                ctx.op("vector", lambda v, dst=dst, hh=hh: v.tensor_scalar(
                    out=dst, in0=qks, scalar1=qkw[:, hh:hh + 1], scalar2=qkb[:, hh:hh + 1],
                    op0=OP.mult, op1=OP.add))
            ctx.op("vector", lambda v: v.tensor_scalar(out=lkf, in0=qks, scalar1=qkw3s[:, 0:1], scalar2=qkb3s[:, 0:1], op0=OP.mult, op1=OP.add), dve, 1)
            dve_qk2 = dve.n
            ctx.wait("sync", dve, dve_qk2)
            if g >= 1:
                ctx.wait("sync", pe, pe_bgrp[g - 1])
            for i in range(2):
                ctx.op("sync", lambda e, i=i: e.dma_start(
                    out=linktm[:, i * 128:i * 128 + 128],
                    in_=lkf[:, i * 128:i * 128 + 128], transpose=True), sc, 16)
            sc_lkt2 = sc.n
            sc_lkt2_last = sc_lkt2

            ctx.wait("tensor", sc, sc_lkt2)
            kv_ev2 = []
            for hb in range(4):
                b = 5 + (hb % 2)
                if hb >= 2:
                    ctx.wait("tensor", dve, kv_ev2[hb - 2])
                for i in range(2):
                    vb = vring[:, ((2 * g + i) % 4) * HID: ((2 * g + i) % 4) * HID + HID]
                    ctx.op("tensor", lambda t, i=i, hb=hb, b=b, vb=vb: t.matmul(
                        out=PS[b][:, 0:512], lhsT=linktm[:, i * 128:i * 128 + 128],
                        rhs=vb[:, hb * 512:hb * 512 + 512],
                        start=(i == 0), stop=(i == 1)), pe, 1)
                pe_kv2 = pe.n
                ctx.wait("vector", pe, pe_kv2)
                ctx.op("vector", lambda v, hb=hb, b=b: v.tensor_add(
                    out=kvrun[:, hb * 512:hb * 512 + 512], in0=PS[b][:, 0:512],
                    in1=kvrun[:, hb * 512:hb * 512 + 512]), dve, 1)
                kv_ev2.append(dve.n)
            dve_kv2 = dve.n

            ctx.wait("tensor", dve, dve_kv2)
            for kt in range(2):
                ctx.op("tensor", lambda t, kt=kt: t.matmul(
                    out=PS[4 + kt][:, 0:256], lhsT=quadk[:, kt * 128:kt * 128 + 128],
                    rhs=quadq, start=True, stop=True), pe, 1)
            pe_sim = pe.n
            ctx.wait("vector", pe, pe_sim)
            if g == 0:
                ctx.wait("vector", sc, sc_bt)
            for kt in range(2):
                ctx.op("vector", lambda v, kt=kt: v.scalar_tensor_tensor(
                    out=simy[:, kt * 256:kt * 256 + 256], in0=PS[4 + kt][:, 0:256],
                    scalar=1.0 / GRP, in1=BT[:, kt * 256:kt * 256 + 256],
                    op0=OP.mult, op1=OP.add))
            ctx.op("vector", lambda v: v.tensor_scalar(out=simy, in0=simy, scalar1=0.0, scalar2=None, op0=OP.max))
            ctx.op("vector", lambda v: v.tensor_mul(out=attnT, in0=simy, in1=simy), dve, 1)
            dve_attnT = dve.n
            if g == 0:
                ctx.wait("gpsimd", dve, dve_attnT)
                dbg_dump("attnT0", attnT)

            act_gate_l = []
            dve_outh_l = []
            pe_wout_l = []
            for hc in range(NHC):
                gb = 4 + (hc % 2)
                qb = 6 + (hc % 2)
                gsl = gate_r[:, (hc % 2) * 256:(hc % 2) * 256 + 256]
                osl = outh_r[:, (hc % 2) * 256:(hc % 2) * 256 + 256]
                if hc < 2:
                    ctx.wait("tensor", dve, dve_attnT)
                else:
                    ctx.wait("tensor", act, act_gate_l[hc - 2])
                for c in range(KD):
                    ctx.op("tensor", lambda t, c=c, hc=hc, gb=gb, nrm=nrm: t.matmul(
                        out=PS[gb][:, 0:256],
                        lhsT=whg_sb[:, c * HID + hc * 128: c * HID + hc * 128 + 128],
                        rhs=nrm[:, c * 256:c * 256 + 256],
                        start=(c == 0), stop=(c == KD - 1)), pe, 1)
                pe_gate = pe.n
                if hc < 2:
                    ctx.wait("tensor", dve, dve_kv2)
                else:
                    ctx.wait("tensor", dve, dve_outh_l[hc - 2])
                for kt in range(2):
                    vb = vring[:, ((2 * g + kt) % 4) * HID: ((2 * g + kt) % 4) * HID + HID]
                    ctx.op("tensor", lambda t, kt=kt, hc=hc, qb=qb, vb=vb: t.matmul(
                        out=PS[qb][:, 0:256], lhsT=vb[:, hc * 128:hc * 128 + 128],
                        rhs=attnT[:, kt * 256:kt * 256 + 256],
                        start=(kt == 0), stop=False), pe, 1)
                ctx.op("tensor", lambda t, hc=hc, qb=qb: t.matmul(
                    out=PS[qb][:, 0:256], lhsT=kvbf[:, hc * 128:hc * 128 + 128],
                    rhs=linq, start=False, stop=True), pe, 1)
                pe_ql = pe.n
                ctx.wait("scalar", pe, pe_gate)
                if hc >= 2:
                    ctx.wait("scalar", dve, dve_outh_l[hc - 2])
                ctx.op("scalar", lambda s, hc=hc, gb=gb, gsl=gsl: s.activation(
                    out=gsl, in_=PS[gb][:, 0:256], func=AF.Silu, bias=bhg[:, hc:hc + 1]), act, 1)
                act_gate_l.append(act.n)
                ctx.wait("vector", pe, pe_ql)
                ctx.wait("vector", act, act_gate_l[-1])
                if hc >= 2:
                    ctx.wait("vector", pe, pe_wout_l[hc - 2])
                ctx.op("vector", lambda v, qb=qb, gsl=gsl, osl=osl: v.tensor_mul(
                    out=osl, in0=gsl, in1=PS[qb][:, 0:256]), dve, 1)
                dve_outh_l.append(dve.n)
                ctx.wait("tensor", dve, dve_outh_l[-1])
                for tcl in range(2):
                    for oh2 in range(2):
                        ctx.op("tensor", lambda t, hc=hc, tcl=tcl, oh2=oh2, osl=osl: t.matmul(
                            out=PS[2 * tcl + oh2][:, 0:512],
                            lhsT=osl[:, tcl * 128:tcl * 128 + 128],
                            rhs=wout_sb[:, hc * 1024 + oh2 * 512: hc * 1024 + oh2 * 512 + 512],
                            start=(hc == 0), stop=(hc == NHC - 1)), pe, 1)
                pe_wout_l.append(pe.n)

            pe_grp = pe.n
            pe_bgrp.append(pe_grp)
            ctx.wait("vector", pe, pe_grp)
            for tcl in range(2):
                tcg = 2 * g + tcl
                for oh2 in range(2):
                    accsl = ACC[:, tcg * 1024 + oh2 * 512: tcg * 1024 + oh2 * 512 + 512]
                    ctx.op("vector", lambda v, tcl=tcl, oh2=oh2, accsl=accsl: v.tensor_add(
                        out=accsl, in0=PS[2 * tcl + oh2][:, 0:512], in1=accsl), dve, 1)
            dve_grp = dve.n

            if g + 2 < NGRP:
                ctx.wait("gpsimd", dve, dve_grp)
                emit_group_loads(g + 2)

            ctx.wait("sync", dve, dve_grp)
            for tcl in range(2):
                tcg = 2 * g + tcl
                ctx.op("sync", lambda e, tcg=tcg: e.dma_start(
                    out=d_out[tcg * 128:(tcg + 1) * 128, :],
                    in_=ACC[:, tcg * 1024:(tcg + 1) * 1024]), out_s, 16)

        if "ACC_fin" in d_dbg:
            ctx.wait("gpsimd", dve, dve.n)
            dbg_dump("ACC_fin", ACC)

        ctx.wait("sync", out_s, out_s.n)
        ctx.wait("sync", sp, sp.n)

    block.sync(lambda e: program(Ctx("sync", e)))
    block.scalar(lambda e: program(Ctx("scalar", e)))
    block.vector(lambda e: program(Ctx("vector", e)))
    block.tensor(lambda e: program(Ctx("tensor", e)))
    block.gpsimd(lambda e: program(Ctx("gpsimd", e)))

    es.close()
    return nc


def kernel(**inputs):
    oh = _host_consts()
    xf = np.ascontiguousarray(np.asarray(inputs["L0_attn_x"], dtype=np.float32).reshape(16384, DIM))
    base = {
        "ln_w": np.ascontiguousarray(inputs["L0_attn_ln_w"], dtype=np.float32),
        "ln_b": np.ascontiguousarray(inputs["L0_attn_ln_b"], dtype=np.float32),
        "Wh": np.ascontiguousarray(inputs["L0_attn_Wh"], dtype=np.float32),
        "bh": np.ascontiguousarray(inputs["L0_attn_bh"], dtype=np.float32),
        "Wqk": np.ascontiguousarray(inputs["L0_attn_Wqk"], dtype=np.float32),
        "bqk": np.ascontiguousarray(inputs["L0_attn_bqk"], dtype=np.float32),
        "qk_w": np.ascontiguousarray(inputs["L0_attn_qk_w"], dtype=np.float32),
        "qk_b": np.ascontiguousarray(inputs["L0_attn_qk_b"], dtype=np.float32),
        "rel_table": np.ascontiguousarray(inputs["L0_attn_rel_table"], dtype=np.float32),
        "Wout": np.ascontiguousarray(inputs["L0_attn_Wout"], dtype=np.float32),
        "bout": np.ascontiguousarray(inputs["L0_attn_bout"], dtype=np.float32),
        "Wg": np.ascontiguousarray(inputs["L0_ffn_Wg"], dtype=np.float32),
        "Ws1": np.ascontiguousarray(inputs["L0_ffn_Ws1"], dtype=np.float32),
        "Ws3": np.ascontiguousarray(inputs["L0_ffn_Ws3"], dtype=np.float32),
        "Ws2": np.ascontiguousarray(inputs["L0_ffn_Ws2"], dtype=np.float32),
        "We1": np.ascontiguousarray(inputs["L0_ffn_We1"], dtype=np.float32),
        "We3": np.ascontiguousarray(inputs["L0_ffn_We3"], dtype=np.float32),
        "We2": np.ascontiguousarray(inputs["L0_ffn_We2"], dtype=np.float32),
        "ohm": oh,
    }
    in_maps = []
    for c in range(NCORES):
        m = dict(base)
        m["x"] = np.ascontiguousarray(xf[c * TOK:(c + 1) * TOK])
        sm = np.zeros((1, 8), dtype=np.float32)
        for r in range(NCORES):
            if r // 4 == c // 4 and r < c:
                sm[0, r] = 1.0
        m["smask"] = sm
        in_maps.append(m)

    nc = build_program(n_debug=tuple(DEBUG.keys()))
    res = run_bass_kernel_spmd(nc, in_maps, core_ids=list(range(NCORES)))
    out = np.concatenate([np.asarray(r["out"]) for r in res.results], axis=0)
    if DEBUG:
        kernel.debug_results = res.results
    return out.reshape(2, 8192, DIM)


if __name__ == "__main__":
    nc = build_program()
    print("built ok")


# version 20
# speedup vs baseline: 1.0184x; 1.0098x over previous
import math
import contextlib
import numpy as np

import concourse.bass as bass
import concourse.mybir as mybir
from concourse.ap import AP
from concourse.bass_utils import run_bass_kernel_spmd

F32 = mybir.dt.float32
BF16 = mybir.dt.bfloat16
AF = mybir.ActivationFunctionType
OP = mybir.AluOpType
AX = mybir.AxisListType.X

DIM = 1024
HID = 2048
QK = 128
GRP = 256
NBUCK = 32
MAXD = 128
BSCALE = float(QK) ** 0.5
EPS = 1e-5
NCORES = 8
TOK = 2048
NTC = 16
NGRP = 8
KD = 8
NHC = 16
NEG = -1e9

DEBUG = {}


def _bucket_of(n):
    max_exact = NBUCK // 2
    ln_ratio = math.log(MAXD / max_exact)
    if n < max_exact:
        return n
    v = max_exact + int(math.log(max(n, 1) / max_exact) / ln_ratio * (NBUCK - max_exact))
    return min(v, NBUCK - 1)


def _host_consts():
    oh = np.zeros((NBUCK, GRP), dtype=np.float32)
    for d in range(GRP):
        oh[_bucket_of(d), d] = 1.0
    return oh


class Sem:
    def __init__(self, handle):
        self.h = handle
        self.n = 0


class Ctx:

    def __init__(self, cur, engine):
        self.cur = cur
        self.e = engine

    def op(self, name, f, inc=None, incn=1):
        if inc is not None:
            inc.n += incn
        if self.cur == name:
            inst = f(self.e)
            if inc is not None:
                inst.then_inc(inc.h, incn)

    def wait(self, name, sem, val):
        assert val <= sem.n, f"wait for {val} but ledger at {sem.n}"
        if self.cur == name:
            self.e.wait_ge(sem.h, val)


SB_BASE = 18304


def _sbuf_map():
    m = {}
    cur = SB_BASE

    def a(name, nbytes, align=32):
        nonlocal cur
        cur = (cur + align - 1) // align * align
        m[name] = cur
        cur += nbytes

    a("lnw", 32); a("lnb", 32)
    a("bqk", 4)
    a("qkw", 16); a("qkb", 16)
    a("qkw3s", 4); a("qkb3s", 4)
    a("bhg", 64)
    a("wgf", 256)
    a("wghF", 128); a("wglF", 128)
    a("eps", 4)
    a("maskb", 32)
    a("relf", 4); a("relhi", 2); a("rello", 2)
    a("BT", 2048)
    a("logits", 512)
    a("wtm", 576)
    a("stats", 384)
    a("wqk", 2048)
    a("xnlo", 4096)
    a("rings", 3584)
    a("xnF", 32768)
    a("ACC", 65536)
    a("normF", 32768)
    m["BIG"] = (cur + 31) // 32 * 32
    m["_big_size"] = 229376 - m["BIG"]
    assert m["_big_size"] >= 64 * 1024, f"BIG too small: {m['_big_size']}"
    return m


DEBUG_TAPS = {
    "wtm": ((128, 144), F32),
    "kvacc": ((128, HID), F32),
    "kvbase": ((128, HID), F32),
    "ACC_ffn": ((128, NTC * 1024), F32),
    "xnF": ((128, NTC * 1024), BF16),
    "normF": ((128, KD * TOK), BF16),
    "v0": ((128, HID), BF16),
    "logits": ((128, 128), F32),
    "BT": ((128, 512), F32),
    "attnT0": ((128, 512), BF16),
    "ACC_fin": ((128, NTC * 1024), F32),
    "x0": ((128, 1024), F32),
    "xn0": ((128, 1024), F32),
    "xbf0": ((128, 1024), BF16),
    "xpx0": ((128, 1024), F32),
    "xpx9": ((128, 1024), F32),
}


def build_program(n_debug=()):
    nc = bass.Bass("TRN2", target_bir_lowering=False, debug=False)

    d_x = nc.dram_tensor("x", [TOK, DIM], F32, kind="ExternalInput").ap()
    d_lnw = nc.dram_tensor("ln_w", [DIM], F32, kind="ExternalInput").ap()
    d_lnb = nc.dram_tensor("ln_b", [DIM], F32, kind="ExternalInput").ap()
    d_Wh = nc.dram_tensor("Wh", [DIM, 2 * HID], F32, kind="ExternalInput").ap()
    d_bh = nc.dram_tensor("bh", [2 * HID], F32, kind="ExternalInput").ap()
    d_Wqk = nc.dram_tensor("Wqk", [DIM, QK], F32, kind="ExternalInput").ap()
    d_bqk = nc.dram_tensor("bqk", [QK], F32, kind="ExternalInput").ap()
    d_qkw = nc.dram_tensor("qk_w", [4, QK], F32, kind="ExternalInput").ap()
    d_qkb = nc.dram_tensor("qk_b", [4, QK], F32, kind="ExternalInput").ap()
    d_rel = nc.dram_tensor("rel_table", [NBUCK], F32, kind="ExternalInput").ap()
    d_Wout = nc.dram_tensor("Wout", [HID, DIM], F32, kind="ExternalInput").ap()
    d_bout = nc.dram_tensor("bout", [DIM], F32, kind="ExternalInput").ap()
    d_Wg = nc.dram_tensor("Wg", [DIM, 8], F32, kind="ExternalInput").ap()
    d_Ws1 = nc.dram_tensor("Ws1", [DIM, HID], F32, kind="ExternalInput").ap()
    d_Ws3 = nc.dram_tensor("Ws3", [DIM, HID], F32, kind="ExternalInput").ap()
    d_Ws2 = nc.dram_tensor("Ws2", [HID, DIM], F32, kind="ExternalInput").ap()
    d_We1 = nc.dram_tensor("We1", [8, DIM, HID], F32, kind="ExternalInput").ap()
    d_We3 = nc.dram_tensor("We3", [8, DIM, HID], F32, kind="ExternalInput").ap()
    d_We2 = nc.dram_tensor("We2", [8, HID, DIM], F32, kind="ExternalInput").ap()
    d_oh = nc.dram_tensor("ohm", [NBUCK, GRP], F32, kind="ExternalInput").ap()
    d_sm = nc.dram_tensor("smask", [1, 8], F32, kind="ExternalInput").ap()
    d_out = nc.dram_tensor("out", [TOK, DIM], F32, kind="ExternalOutput").ap()

    d_text = nc.dram_tensor("t_ext", [512], F32).ap()
    d_xpx = nc.dram_tensor("xpx_spill", [NTC, 128, DIM], F32).ap()
    d_vsp = nc.dram_tensor("v_spill", [NTC, 128, HID], BF16).ap()
    d_nsp = nc.dram_tensor("n_spill", [NGRP, 128, KD * GRP], BF16).ap()
    d_ccin = nc.dram_tensor("cc_in", [QK, HID], F32).ap()
    d_ccout = nc.dram_tensor("cc_out", [NCORES * QK, HID], F32).ap()

    d_dbg = {}
    for nm in n_debug:
        shp, dt = DEBUG_TAPS[nm]
        d_dbg[nm] = nc.dram_tensor("dbg_" + nm, list(shp), dt, kind="ExternalOutput").ap()

    M = _sbuf_map()

    def sb(name, shape, dtype, off=None):
        off = M[name] if off is None else off
        return nc.alloc_sbuf_tensor_at(name, list(shape), dtype, offset=off).ap()

    lnw = sb("lnw", [128, 8], F32); lnb = sb("lnb", [128, 8], F32)
    bqk = sb("bqk", [128, 1], F32)
    qkw = sb("qkw", [128, 4], F32); qkb = sb("qkb", [128, 4], F32)
    qkw3s = sb("qkw3s", [128, 1], F32); qkb3s = sb("qkb3s", [128, 1], F32)
    bhg = sb("bhg", [128, 16], F32)
    wgf = sb("wgf", [128, 64], F32)
    wghF = sb("wghF", [128, 64], BF16)
    wglF = sb("wglF", [128, 64], BF16)
    epsc = sb("eps", [128, 1], F32)
    maskb = sb("maskb", [128, 8], F32)
    relf = sb("relf", [32, 1], F32)
    relhi = sb("relhi", [32, 1], BF16); rello = sb("rello", [32, 1], BF16)
    BT = sb("BT", [128, 512], F32)
    logits = sb("logits", [128, 128], F32)
    wtm = sb("wtm", [128, 144], F32)
    stats = sb("stats", [128, 96], F32)
    wqk = sb("wqk", [128, 1024], BF16)
    xnlo = sb("xnlo", [128, 2048], BF16)

    R = M["rings"]
    ohb = sb("ohb", [32, 256], BF16, off=R)
    trow = sb("trow", [1, 256], F32, off=R + 512)
    negrow = sb("negrow", [1, 256], F32, off=R + 1536)
    qks = sb("qks", [128, 256], F32, off=R)
    lkf = sb("lkf", [128, 256], BF16, off=R + 1024)
    linktm = sb("linktm", [128, 256], BF16, off=R + 1536)
    quadq = sb("quadq", [128, 256], BF16, off=R + 2048)
    quadk = sb("quadk", [128, 256], BF16, off=R + 2560)
    linq = sb("linq", [128, 256], BF16, off=R + 3072)
    sA = sb("sA", [128, 512], F32, off=R)
    pf = sb("pf", [128, 512], BF16, off=R + 2048)

    xnF = sb("xnF", [128, NTC * 1024], BF16)
    ACC = sb("ACC", [128, NTC * 1024], F32)
    normF = sb("normF", [128, KD * TOK], BF16)

    BIG = M["BIG"]
    xb = sb("xb", [128, 2048], F32, off=BIG)
    xcb = sb("xcb", [128, 2048], F32, off=BIG + 8192)
    sqb = sb("sqb", [128, 1024], F32, off=BIG + 16384)
    xnb = sb("xnb", [128, 2048], F32, off=BIG + 20480)
    xpxb = sb("xpxb", [128, 2048], F32, off=BIG + 28672)
    nrawb = sb("nrawb", [128, 2048], BF16, off=BIG + 36864)
    xnbf = sb("xnbf", [128, 2048], BF16, off=BIG + 40960)
    xnlob = sb("xnlob", [128, 2048], BF16, off=BIG + 45056)
    boutbc = sb("boutbc", [128, 1024], F32, off=BIG + 49152)

    wset = [
        {"w1": sb("w1_0", [128, 8192], BF16, off=BIG),
         "w3": sb("w3_0", [128, 8192], BF16, off=BIG + 16384),
         "w2": sb("w2_0", [128, 8192], BF16, off=BIG + 32768)},
        {"w1": sb("w1_1", [128, 8192], BF16, off=BIG + 49152),
         "w3": sb("w3_1", [128, 8192], BF16, off=M["normF"]),
         "w2": sb("w2_1", [128, 8192], BF16, off=M["normF"] + 16384)},
    ]
    gate_r = sb("gate_r", [128, 512], F32, off=BIG + 49152)
    outh_r = sb("outh_r", [128, 512], BF16, off=BIG + 51200)
    attnT = sb("attnT", [128, 512], BF16, off=BIG + 52224)
    simy = sb("simy", [128, 512], F32, off=BIG + 53248)
    fill = sb("fill", [128, 256], F32, off=BIG + 55296)
    xnlo_b = sb("xnlo_b", [128, 2048], BF16, off=BIG + 56320)

    AC = M["ACC"]
    whv = sb("whv", [128, KD * HID], BF16, off=AC)
    bhvbc = sb("bhvbc", [128, HID], F32, off=AC + 32768)
    vbufA = sb("vbufA", [128, 2 * HID], BF16, off=AC + 40960)
    kvaccA = sb("kvaccA", [128, HID], F32, off=AC + 49152)
    vtmp = sb("vtmp", [128, 1024], F32, off=AC + 57344)

    wout_sb = sb("wout_sb", [128, NHC * 1024], BF16, off=BIG)
    vring = sb("vring", [128, 4 * HID], BF16, off=BIG + 32768)
    whg_sb = sb("whg_sb", [128, KD * HID], BF16, off=M["xnF"])
    normg = sb("normg", [128, 2 * KD * GRP], BF16, off=M["normF"])
    kvrun = sb("kvrun", [128, HID], F32, off=M["normF"] + 8192)
    kvbf = sb("kvbf", [128, HID], BF16, off=M["normF"] + 16384)
    ccr = sb("ccr", [128, HID], F32, off=M["normF"] + 20480)

    PS = [nc.alloc_psum_tensor(f"ps{i}", [128, 512], F32).ap() for i in range(8)]

    es = contextlib.ExitStack()
    sem_names = ["ld", "sp", "sc", "pe", "act", "dve", "cc", "out"]
    handles = {nm: es.enter_context(nc.semaphore("s_" + nm)) for nm in sem_names}
    block = es.enter_context(nc.Block())

    def program(ctx):
        S = {nm: Sem(handles[nm]) for nm in sem_names}
        ld, sp, sc, pe, act, dve, cc, out_s = (
            S["ld"], S["sp"], S["sc"], S["pe"], S["act"], S["dve"], S["cc"], S["out"])

        def dbg_dump(name, src_ap):
            if name in d_dbg:
                ctx.op("gpsimd", lambda g: g.dma_start(out=d_dbg[name], in_=src_ap), sp, 16)

        ctx.op("sync", lambda e: e.dma_start(out=relf, in_=d_rel[:, None]), ld, 16)
        ctx.op("sync", lambda e: e.dma_start(out=bqk, in_=d_bqk[:, None]), ld, 16)
        for h in range(4):
            ctx.op("sync", lambda e, h=h: e.dma_start(out=qkw[:, h:h + 1], in_=d_qkw[h, :][:, None]), ld, 16)
            ctx.op("sync", lambda e, h=h: e.dma_start(out=qkb[:, h:h + 1], in_=d_qkb[h, :][:, None]), ld, 16)
        for c in range(8):
            ctx.op("sync", lambda e, c=c: e.dma_start(out=lnw[:, c:c + 1], in_=d_lnw[c * 128:(c + 1) * 128][:, None]), ld, 16)
            ctx.op("sync", lambda e, c=c: e.dma_start(out=lnb[:, c:c + 1], in_=d_lnb[c * 128:(c + 1) * 128][:, None]), ld, 16)
        for c in range(16):
            ctx.op("sync", lambda e, c=c: e.dma_start(out=bhg[:, c:c + 1], in_=d_bh[HID + c * 128: HID + (c + 1) * 128][:, None]), ld, 16)
        ctx.op("sync", lambda e: e.dma_start(
            out=wgf.rearrange("p (c n) -> p c n", n=8),
            in_=d_Wg.rearrange("(c p) n -> p c n", p=128)), ld, 16)
        ld_pro = ld.n

        ctx.op("gpsimd", lambda g: g.dma_start(out=ohb, in_=d_oh), sp, 16)
        ctx.op("gpsimd", lambda g: g.dma_start(out=maskb, in_=d_sm.broadcast_to([128, 8])), sp, 16)
        ctx.op("gpsimd", lambda g: g.dma_start(out=boutbc, in_=d_bout[None, :].broadcast_to([128, DIM])), sp, 16)
        ctx.op("gpsimd", lambda g: g.dma_start(out=bhvbc, in_=d_bh[None, 0:HID].broadcast_to([128, HID])), sp, 16)
        sp_bc = sp.n
        ctx.wait("gpsimd", sp, sp_bc)
        ctx.op("gpsimd", lambda g: g.dma_start(
            out=wqk.rearrange("p (c n) -> p c n", n=128),
            in_=d_Wqk.rearrange("(c p) n -> p c n", p=128)), sp, 16)
        ctx.op("gpsimd", lambda g: g.dma_start(
            out=whv.rearrange("p (c n) -> p c n", n=HID),
            in_=d_Wh.rearrange("(c p) n -> p c n", p=128)[:, :, 0:HID]), sp, 16)
        sp_wts = sp.n

        ctx.wait("vector", ld, ld_pro)
        ctx.wait("vector", sp, sp_bc)
        ctx.op("vector", lambda v: v.memset(epsc, EPS))
        ctx.op("vector", lambda v: v.memset(negrow, NEG))
        ctx.op("vector", lambda v: v.memset(kvaccA, 0.0))
        ctx.op("vector", lambda v: v.tensor_scalar(out=qkw3s, in0=qkw[:, 3:4], scalar1=1.0 / GRP, scalar2=None, op0=OP.mult))
        ctx.op("vector", lambda v: v.tensor_scalar(out=qkb3s, in0=qkb[:, 3:4], scalar1=1.0 / GRP, scalar2=None, op0=OP.mult))
        ctx.op("vector", lambda v: v.tensor_scalar(out=relf, in0=relf, scalar1=BSCALE, scalar2=None, op0=OP.mult))
        ctx.op("vector", lambda v: v.tensor_copy(out=wghF, in_=wgf))
        ctx.op("vector", lambda v: v.tensor_copy(out=relhi, in_=relf))
        ctx.op("vector", lambda v: v.tensor_sub(out=wglF, in0=wgf, in1=wghF))
        ctx.op("vector", lambda v: v.tensor_sub(out=rello, in0=relf, in1=relhi))
        ctx.op("vector", lambda v: v.tensor_copy(out=fill[:, 0:128], in_=fill[:, 128:256]), dve, 1)
        dve_pro = dve.n

        ctx.wait("tensor", dve, dve_pro)
        ctx.op("tensor", lambda t: t.matmul(out=PS[7][0:1, 0:256], lhsT=relhi, rhs=ohb, start=True, stop=False), pe, 1)
        ctx.op("tensor", lambda t: t.matmul(out=PS[7][0:1, 0:256], lhsT=rello, rhs=ohb, start=False, stop=True), pe, 1)
        pe_bt = pe.n
        ctx.wait("vector", pe, pe_bt)
        ctx.op("vector", lambda v: v.tensor_copy(out=trow, in_=PS[7][0:1, 0:256]))
        ctx.op("vector", lambda v: v.tensor_copy(out=fill[:, 0:128], in_=fill[:, 128:256]), dve, 1)
        dve_bt = dve.n
        ctx.wait("gpsimd", dve, dve_bt)
        ctx.op("gpsimd", lambda g: g.dma_start(out=d_text[None, 0:256], in_=negrow), sp, 16)
        ctx.op("gpsimd", lambda g: g.dma_start(out=d_text[None, 256:512], in_=trow), sp, 16)
        sp_text = sp.n

        for tc0 in range(2):
            ctx.op("sync", lambda e, tc=tc0: e.dma_start(
                out=xb[:, (tc % 2) * 1024:(tc % 2) * 1024 + 1024],
                in_=d_x[tc * 128:(tc + 1) * 128, :]), ld, 16)
        ld_x = [ld_pro + 16, ld_pro + 32]

        act_sqrt2 = []
        dve_xpx_l = []
        dve_ln = []
        dve_nsc = []
        dve_rtr_evac = []
        pe_rtr = []
        sc_tr = []
        sp_xpx = []
        act_vev = []
        sp_vsp = []
        dve_kv = []
        dve_vt = []
        state = {"dve_lkf_last": dve_pro, "sc_lkt": None, "pe_kv_last": None,
                 "act_qk_last": None}

        def spc(v):
            return v.tensor_copy(out=fill[:, 0:128], in_=fill[:, 128:256])

        def xnlo_slot(tc):
            s4 = tc % 4
            base = xnlo if s4 < 2 else xnlo_b
            return base[:, (s4 % 2) * 1024:(s4 % 2) * 1024 + 1024]

        def emit_ln(tc):
            X = xb[:, (tc % 2) * 1024:(tc % 2) * 1024 + 1024]
            XN = xnb[:, (tc % 2) * 1024:(tc % 2) * 1024 + 1024]
            XP = xpxb[:, (tc % 2) * 1024:(tc % 2) * 1024 + 1024]
            NR = nrawb[:, (tc % 2) * 1024:(tc % 2) * 1024 + 1024]
            XBF = xnbf[:, (tc % 2) * 1024:(tc % 2) * 1024 + 1024]
            XLO = xnlob[:, (tc % 2) * 1024:(tc % 2) * 1024 + 1024]
            st = stats[:, (tc % 2) * 16:(tc % 2) * 16 + 16]

            def ln_stats(SRC, o):
                ctx.op("vector", lambda v: v.tensor_mul(out=sqb[:, 0:1024], in0=SRC, in1=SRC))
                ctx.op("vector", lambda v: v.reduce_sum(out=st[:, o:o + 1], in_=SRC, axis=AX))
                ctx.op("vector", lambda v: v.reduce_sum(out=st[:, o + 1:o + 2], in_=sqb[:, 0:1024], axis=AX))
                ctx.op("vector", lambda v: v.tensor_scalar(out=st[:, o:o + 1], in0=st[:, o:o + 1], scalar1=1.0 / DIM, scalar2=None, op0=OP.mult))
                ctx.op("vector", lambda v: v.tensor_scalar(out=st[:, o + 1:o + 2], in0=st[:, o + 1:o + 2], scalar1=1.0 / DIM, scalar2=None, op0=OP.mult))
                ctx.op("vector", lambda v: v.tensor_mul(out=st[:, o + 2:o + 3], in0=st[:, o:o + 1], in1=st[:, o:o + 1]))
                ctx.op("vector", spc)
                ctx.op("vector", lambda v: v.tensor_sub(out=st[:, o + 3:o + 4], in0=st[:, o + 1:o + 2], in1=st[:, o + 2:o + 3]))
                ctx.op("vector", spc, dve, 1)
                ctx.wait("scalar", dve, dve.n)
                ctx.op("scalar", lambda s: s.nop(cycle_cnt=800))
                ctx.op("scalar", lambda s: s.activation(out=st[:, o + 4:o + 5], in_=st[:, o + 3:o + 4], func=AF.Sqrt, bias=epsc[:, 0:1]), act, 1)
                ctx.wait("vector", act, act.n)
                ctx.op("vector", lambda v: v.nop(cycle_cnt=800))
                ctx.op("vector", lambda v: v.reciprocal(out=st[:, o + 5:o + 6], in_=st[:, o + 4:o + 5]))
                ctx.op("vector", spc)

            ctx.wait("vector", ld, ld_x[tc])
            if tc >= 2:
                ctx.wait("vector", sc, sc_tr[tc - 2])
                ctx.wait("vector", act, act_sqrt2[tc - 2])
                ctx.wait("vector", sp, sp_xpx[tc - 2])
            ln_stats(X, 0)
            ctx.op("vector", lambda v: v.tensor_scalar(out=XN, in0=X, scalar1=st[:, 0:1], scalar2=st[:, 5:6], op0=OP.subtract, op1=OP.mult))
            ctx.op("vector", lambda v: v.tensor_add(out=XP, in0=X, in1=XN))
            ctx.op("vector", lambda v: v.tensor_add(out=XP, in0=XP, in1=boutbc), dve, 1)
            dve_xpx_l.append(dve.n)
            ln_stats(XN, 8)
            act_sqrt2.append(act.n)
            ctx.op("vector", lambda v: v.tensor_scalar(out=NR, in0=XN, scalar1=st[:, 8:9], scalar2=st[:, 13:14], op0=OP.subtract, op1=OP.mult))
            ctx.op("vector", lambda v: v.tensor_copy(out=XBF, in_=XN))
            ctx.op("vector", lambda v: v.tensor_sub(out=XLO, in0=XN, in1=XBF), dve, 1)
            dve_ln.append(dve.n)

            ctx.wait("gpsimd", dve, dve_xpx_l[-1])
            ctx.op("gpsimd", lambda g_, tc=tc, XP=XP: g_.dma_start(out=d_xpx[tc], in_=XP), sp, 16)
            sp_xpx.append(sp.n)

            if tc + 2 < NTC:
                ctx.wait("sync", dve, dve_xpx_l[-1])
                ctx.op("sync", lambda e, tc=tc: e.dma_start(
                    out=xb[:, (tc % 2) * 1024:(tc % 2) * 1024 + 1024],
                    in_=d_x[(tc + 2) * 128:(tc + 3) * 128, :]), ld, 16)
                ld_x.append(ld.n)

            ctx.wait("sync", dve, dve_ln[-1])
            ctx.wait("scalar", dve, dve_ln[-1])
            if tc >= 4:
                ctx.wait("scalar", pe, pe_rtr[tc - 4])
            XLOD = xnlo_slot(tc)
            for c in range(8):
                ctx.op("sync", lambda e, c=c, tc=tc, NR=NR: e.dma_start(
                    out=normF[:, c * TOK + tc * 128: c * TOK + tc * 128 + 128],
                    in_=NR[:, c * 128:c * 128 + 128], transpose=True), sc, 16)
            sc_nrm = sc.n
            for c in range(8):
                q = "sync" if c % 2 == 0 else "scalar"
                ctx.op(q, lambda e, c=c, tc=tc, XBF=XBF: e.dma_start(
                    out=xnF[:, c * TOK + tc * 128: c * TOK + tc * 128 + 128],
                    in_=XBF[:, c * 128:c * 128 + 128], transpose=True), sc, 16)
                ctx.op("scalar", lambda e, c=c, tc=tc, XLO=XLO, XLOD=XLOD: e.dma_start(
                    out=XLOD[:, c * 128:c * 128 + 128],
                    in_=XLO[:, c * 128:c * 128 + 128], transpose=True), sc, 16)
            sc_tr.append(sc.n)

            ctx.wait("vector", sc, sc_tr[-1])
            for c in range(8):
                ctx.op("vector", lambda v, c=c, tc=tc: v.tensor_scalar(
                    out=normF[:, c * TOK + tc * 128: c * TOK + tc * 128 + 128],
                    in0=normF[:, c * TOK + tc * 128: c * TOK + tc * 128 + 128],
                    scalar1=lnw[:, c:c + 1], scalar2=lnb[:, c:c + 1],
                    op0=OP.mult, op1=OP.add), dve, 1)
            dve_nsc.append(dve.n)

        def emit_router(tc):
            ctx.wait("tensor", sc, sc_tr[tc])
            ctx.wait("tensor", dve, dve_pro)
            if tc >= 1:
                ctx.wait("tensor", dve, dve_rtr_evac[tc - 1])
            if dve_kv:
                ctx.wait("tensor", dve, dve_kv[-1])
            for c in range(8):
                ctx.op("tensor", lambda t, c=c, tc=tc: t.matmul(
                    out=PS[6][:, 0:8], lhsT=xnF[:, c * TOK + tc * 128: c * TOK + tc * 128 + 128],
                    rhs=wghF[:, c * 8:c * 8 + 8], start=(c == 0), stop=False), pe, 1)
            for c in range(8):
                ctx.op("tensor", lambda t, c=c, tc=tc: t.matmul(
                    out=PS[6][:, 0:8], lhsT=xnF[:, c * TOK + tc * 128: c * TOK + tc * 128 + 128],
                    rhs=wglF[:, c * 8:c * 8 + 8], start=False, stop=False), pe, 1)
            XLOD = xnlo_slot(tc)
            for c in range(8):
                ctx.op("tensor", lambda t, c=c, tc=tc, XLOD=XLOD: t.matmul(
                    out=PS[6][:, 0:8], lhsT=XLOD[:, c * 128:c * 128 + 128],
                    rhs=wghF[:, c * 8:c * 8 + 8], start=False, stop=(c == 7)), pe, 1)
            pe_rtr.append(pe.n)
            ctx.wait("vector", pe, pe.n)
            ctx.op("vector", lambda v, tc=tc: v.tensor_copy(out=logits[:, tc * 8:tc * 8 + 8], in_=PS[6][:, 0:8]), dve, 1)
            dve_rtr_evac.append(dve.n)

        def emit_attnA(g):
            ctx.wait("tensor", dve, dve_nsc[2 * g + 1])
            ctx.wait("tensor", sp, sp_wts)
            if state["act_qk_last"] is not None:
                ctx.wait("tensor", act, state["act_qk_last"])
            for c in range(KD):
                ctx.op("tensor", lambda t, c=c, g=g: t.matmul(
                    out=PS[4][:, 0:256], lhsT=wqk[:, c * 128:c * 128 + 128],
                    rhs=normF[:, c * TOK + g * 256: c * TOK + g * 256 + 256],
                    start=(c == 0), stop=(c == KD - 1)), pe, 1)
            pe_qk = pe.n
            ctx.wait("scalar", pe, pe_qk)
            if g > 0:
                ctx.wait("scalar", dve, state["dve_lkf_last"])
                ctx.wait("scalar", sc, state["sc_lkt"])
            ctx.op("scalar", lambda s: s.activation(out=qks, in_=PS[4][:, 0:256], func=AF.Silu, bias=bqk[:, 0:1]), act, 1)
            act_qk = act.n
            state["act_qk_last"] = act_qk
            ctx.wait("vector", act, act_qk)
            if g > 0:
                ctx.wait("vector", sc, state["sc_lkt"])
            ctx.op("vector", lambda v: v.tensor_scalar(out=lkf, in0=qks, scalar1=qkw3s[:, 0:1], scalar2=qkb3s[:, 0:1], op0=OP.mult, op1=OP.add), dve, 1)
            dve_lkf = dve.n
            state["dve_lkf_last"] = dve_lkf
            ctx.wait("sync", dve, dve_lkf)
            if g > 0:
                ctx.wait("sync", pe, state["pe_kv_last"])
            for i in range(2):
                ctx.op("sync", lambda e, i=i: e.dma_start(
                    out=linktm[:, i * 128:i * 128 + 128],
                    in_=lkf[:, i * 128:i * 128 + 128], transpose=True), sc, 16)
            state["sc_lkt"] = sc.n

            for i in range(2):
                tc = 2 * g + i
                vb = vbufA[:, (tc % 2) * HID:(tc % 2) * HID + HID]
                for hb in range(4):
                    b = (i * 4 + hb) % 4
                    if len(dve_vt) >= 4:
                        ctx.wait("tensor", dve, dve_vt[-4])
                    if len(sp_vsp) >= 2 and hb == 0:
                        ctx.wait("tensor", sp, sp_vsp[-2])
                    for c in range(KD):
                        ctx.op("tensor", lambda t, c=c, tc=tc, hb=hb, b=b: t.matmul(
                            out=PS[b][:, 0:512],
                            lhsT=normF[:, c * TOK + tc * 128: c * TOK + tc * 128 + 128],
                            rhs=whv[:, c * HID + hb * 512: c * HID + hb * 512 + 512],
                            start=(c == 0), stop=(c == KD - 1)), pe, 1)
                    pe_v = pe.n
                    ctx.wait("vector", pe, pe_v)
                    if len(act_vev) >= 2:
                        ctx.wait("vector", act, act_vev[-2])
                    ctx.op("vector", lambda v, hb=hb, b=b: v.tensor_add(
                        out=vtmp[:, (hb % 2) * 512:(hb % 2) * 512 + 512],
                        in0=PS[b][:, 0:512], in1=bhvbc[:, hb * 512:hb * 512 + 512]), dve, 1)
                    dve_vt.append(dve.n)
                    ctx.wait("scalar", dve, dve_vt[-1])
                    ctx.op("scalar", lambda s, hb=hb, vb=vb: s.activation(
                        out=vb[:, hb * 512:hb * 512 + 512],
                        in_=vtmp[:, (hb % 2) * 512:(hb % 2) * 512 + 512], func=AF.Silu), act, 1)
                    act_vev.append(act.n)
                ctx.wait("gpsimd", act, act_vev[-1])
                ctx.op("gpsimd", lambda gp, tc=tc, vb=vb: gp.dma_start(out=d_vsp[tc], in_=vb), sp, 16)
                sp_vsp.append(sp.n)

            ctx.wait("tensor", sc, state["sc_lkt"])
            ctx.wait("tensor", act, act_vev[-1])
            if dve_kv:
                ctx.wait("tensor", dve, dve_kv[-1])
            if dve_rtr_evac:
                ctx.wait("tensor", dve, dve_rtr_evac[-1])
            kv_ev = []
            for hb in range(4):
                b = 5 + (hb % 2)
                if hb >= 2:
                    ctx.wait("tensor", dve, kv_ev[hb - 2])
                for i in range(2):
                    vb = vbufA[:, ((2 * g + i) % 2) * HID: ((2 * g + i) % 2) * HID + HID]
                    ctx.op("tensor", lambda t, i=i, hb=hb, b=b, vb=vb: t.matmul(
                        out=PS[b][:, 0:512], lhsT=linktm[:, i * 128:i * 128 + 128],
                        rhs=vb[:, hb * 512:hb * 512 + 512],
                        start=(i == 0), stop=(i == 1)), pe, 1)
                pe_kv = pe.n
                ctx.wait("vector", pe, pe_kv)
                ctx.op("vector", lambda v, hb=hb, b=b: v.tensor_add(
                    out=kvaccA[:, hb * 512:hb * 512 + 512], in0=PS[b][:, 0:512],
                    in1=kvaccA[:, hb * 512:hb * 512 + 512]), dve, 1)
                kv_ev.append(dve.n)
            dve_kv.append(dve.n)
            state["pe_kv_last"] = pe.n

        def emit_set_load(s):
            e, h = s // 2, s % 2
            slot = wset[s % 2]
            if e == 0:
                w1src = d_Ws1.rearrange("(c p) n -> p c n", p=128)[:, :, h * 1024:(h + 1) * 1024]
                w3src = d_Ws3.rearrange("(c p) n -> p c n", p=128)[:, :, h * 1024:(h + 1) * 1024]
                w2src = d_Ws2[h * 1024:(h + 1) * 1024, :].rearrange("(f p) n -> p f n", p=128)
            else:
                w1src = d_We1[e - 1].rearrange("(c p) n -> p c n", p=128)[:, :, h * 1024:(h + 1) * 1024]
                w3src = d_We3[e - 1].rearrange("(c p) n -> p c n", p=128)[:, :, h * 1024:(h + 1) * 1024]
                w2src = d_We2[e - 1][h * 1024:(h + 1) * 1024, :].rearrange("(f p) n -> p f n", p=128)
            ctx.op("gpsimd", lambda g: g.dma_start(
                out=slot["w1"].rearrange("p (c n) -> p c n", n=1024), in_=w1src), sp, 16)
            ctx.op("gpsimd", lambda g: g.dma_start(
                out=slot["w3"].rearrange("p (c n) -> p c n", n=1024), in_=w3src), sp, 16)
            ctx.op("gpsimd", lambda g: g.dma_start(
                out=slot["w2"].rearrange("p (f n) -> p f n", n=1024), in_=w2src), sp, 16)
            return sp.n

        sp_sets = []
        sp_nsp_l = []
        for g in range(NGRP):
            emit_ln(2 * g)
            emit_ln(2 * g + 1)
            ctx.wait("gpsimd", dve, dve_nsc[-1])
            for c in range(KD):
                ctx.op("gpsimd", lambda gp, g=g, c=c: gp.dma_start(
                    out=d_nsp[g][:, c * 256:c * 256 + 256],
                    in_=normF[:, c * TOK + g * 256:c * TOK + g * 256 + 256]), sp, 16)
            sp_nsp_l.append(sp.n)
            if g == NGRP - 1:
                ctx.wait("gpsimd", sc, sc_tr[-1])
                sp_sets.append(emit_set_load(0))
            if g >= 1:
                emit_router(2 * g - 2)
                emit_router(2 * g - 1)
                emit_attnA(g - 1)
        emit_router(2 * NGRP - 2)
        emit_router(2 * NGRP - 1)
        emit_attnA(NGRP - 1)
        pe_a_end = pe.n
        dve_ln_done = dve.n
        sc_ln_done = sc.n
        ctx.wait("gpsimd", dve, dve_kv[-1])
        dbg_dump("kvacc", kvaccA)
        dbg_dump("xnF", xnF)
        dbg_dump("normF", normF)

        ctx.wait("sync", sp, sp_text)
        for K in range(256):
            kc, k = K // 128, K % 128
            src = AP(tensor=d_text.tensor, offset=256 - K, ap=[[1, 1], [1, 256]])
            ctx.op("sync", lambda e, kc=kc, k=k, src=src: e.dma_start(
                out=BT[k:k + 1, kc * 256:(kc + 1) * 256], in_=src), sc, 16)
        sc_bt = sc.n
        ctx.wait("gpsimd", sc, sc_bt)
        dbg_dump("BT", BT)

        ctx.wait("scalar", dve, dve_rtr_evac[-1])
        ctx.op("scalar", lambda s: s.activation(out=logits, in_=logits, func=AF.Sigmoid), act, 1)
        act_sig = act.n
        ctx.wait("vector", act, act_sig)
        for pt in range(NTC // 2):
            els = []
            for j in range(2):
                tc = 2 * pt + j
                base = 32 + 32 * j
                els.append(dict(
                    g8=logits[:, tc * 8:tc * 8 + 8],
                    m1=stats[:, base:base + 1], m2=stats[:, base + 1:base + 2],
                    s12=stats[:, base + 2:base + 3],
                    E=stats[:, base + 4:base + 12], G2=stats[:, base + 12:base + 20],
                    w=wtm[:, tc * 9 + 1: tc * 9 + 9]))
            for E_ in els:
                ctx.op("vector", lambda v, e=E_: v.reduce_max(out=e["m1"], in_=e["g8"], axis=AX))
            for E_ in els:
                ctx.op("vector", lambda v, e=E_: v.tensor_scalar(out=e["E"], in0=e["g8"], scalar1=e["m1"], scalar2=None, op0=OP.is_equal))
            for E_ in els:
                ctx.op("vector", lambda v, e=E_: v.scalar_tensor_tensor(out=e["G2"], in0=e["E"], scalar=NEG, in1=e["g8"], op0=OP.mult, op1=OP.add))
            for E_ in els:
                ctx.op("vector", lambda v, e=E_: v.reduce_max(out=e["m2"], in_=e["G2"], axis=AX))
            for E_ in els:
                ctx.op("vector", lambda v, e=E_: v.tensor_scalar(out=e["E"], in0=e["g8"], scalar1=e["m2"], scalar2=None, op0=OP.is_ge))
            for E_ in els:
                ctx.op("vector", lambda v, e=E_: v.tensor_add(out=e["s12"], in0=e["m1"], in1=e["m2"]))
            for E_ in els:
                ctx.op("vector", lambda v, e=E_: v.reciprocal(out=e["s12"], in_=e["s12"]))
            for E_ in els:
                ctx.op("vector", lambda v, e=E_: v.tensor_scalar(out=e["s12"], in0=e["s12"], scalar1=1.0 / 3.0, scalar2=None, op0=OP.mult))
            for E_ in els:
                ctx.op("vector", lambda v, e=E_: v.tensor_mul(out=e["E"], in0=e["E"], in1=e["g8"]))
            for E_ in els:
                ctx.op("vector", lambda v, e=E_: v.tensor_scalar(
                    out=e["w"], in0=e["E"], scalar1=e["s12"], scalar2=None, op0=OP.mult), dve, 1)
        dve_router = dve.n
        ctx.wait("gpsimd", dve, dve_router)
        dbg_dump("wtm", wtm)
        dbg_dump("logits", logits)

        ctx.wait("gpsimd", dve, dve_kv[-1])
        ctx.op("gpsimd", lambda g: g.dma_start(out=d_ccin, in_=kvaccA), sp, 16)
        sp_ccin = sp.n
        ctx.wait("gpsimd", sp, sp_ccin)
        ctx.op("gpsimd", lambda g: g.collective_compute(
            "AllGather", OP.bypass, replica_groups=[list(range(NCORES))],
            ins=[d_ccin.opt()], outs=[d_ccout.opt()]), cc, 1)

        sp_nsp = sp_nsp_l[-1]

        ctx.wait("gpsimd", pe, pe_a_end)
        for tc in range(NTC):
            ctx.op("gpsimd", lambda g, tc=tc: g.dma_start(
                out=ACC[:, tc * 1024:(tc + 1) * 1024], in_=d_xpx[tc]), sp, 16)
        sp_accinit = sp.n

        pe_sets = []
        dve_yev = []
        sp_wout = None
        sp_whg = None

        ctx.wait("gpsimd", sp, sp_nsp)
        ctx.wait("gpsimd", pe, pe_rtr[-1])
        sp_sets.append(emit_set_load(1))

        ctx.wait("vector", sp, sp_accinit)

        for s in range(18):
            e = s // 2
            slot = wset[s % 2]
            ctx.wait("tensor", sp, sp_sets[s])
            if s == 0:
                ctx.wait("tensor", dve, dve_kv[-1])
                ctx.wait("tensor", dve, dve_router)
            for strip in range(NGRP):
                pf_l = []
                silu_l = []

                def emit_y(f):
                    ctx.wait("tensor", dve, pf_l[f])
                    for tcl in range(2):
                        for oh2 in range(2):
                            ctx.op("tensor", lambda t, f=f, tcl=tcl, oh2=oh2: t.matmul(
                                out=PS[2 * tcl + oh2][:, 0:512],
                                lhsT=pf[:, (f % 2) * 256 + tcl * 128:(f % 2) * 256 + tcl * 128 + 128],
                                rhs=slot["w2"][:, f * 1024 + oh2 * 512: f * 1024 + oh2 * 512 + 512],
                                start=(f == 0), stop=(f == 7)), pe, 1)

                for f in range(8):
                    ab = 4 + 2 * (f % 2)
                    if f >= 2:
                        ctx.wait("tensor", act, silu_l[f - 2])
                        ctx.wait("tensor", dve, pf_l[f - 2])
                    for c in range(KD):
                        ctx.op("tensor", lambda t, c=c, f=f, ab=ab, strip=strip: t.matmul(
                            out=PS[ab][:, 0:256],
                            lhsT=slot["w1"][:, c * 1024 + f * 128: c * 1024 + f * 128 + 128],
                            rhs=xnF[:, c * TOK + strip * 256: c * TOK + strip * 256 + 256],
                            start=(c == 0), stop=(c == KD - 1)), pe, 1)
                    for c in range(KD):
                        ctx.op("tensor", lambda t, c=c, f=f, ab=ab, strip=strip: t.matmul(
                            out=PS[ab + 1][:, 0:256],
                            lhsT=slot["w3"][:, c * 1024 + f * 128: c * 1024 + f * 128 + 128],
                            rhs=xnF[:, c * TOK + strip * 256: c * TOK + strip * 256 + 256],
                            start=(c == 0), stop=(c == KD - 1)), pe, 1)
                    pe_ab = pe.n
                    ctx.wait("scalar", pe, pe_ab)
                    ctx.op("scalar", lambda sE, f=f, ab=ab: sE.activation(
                        out=sA[:, (f % 2) * 256:(f % 2) * 256 + 256], in_=PS[ab][:, 0:256], func=AF.Silu), act, 1)
                    silu_l.append(act.n)
                    ctx.wait("vector", act, act.n)
                    ctx.op("vector", lambda v, f=f, ab=ab: v.tensor_mul(
                        out=pf[:, (f % 2) * 256:(f % 2) * 256 + 256],
                        in0=sA[:, (f % 2) * 256:(f % 2) * 256 + 256], in1=PS[ab + 1][:, 0:256]), dve, 1)
                    pf_l.append(dve.n)
                    if f == 0 and dve_yev:
                        ctx.wait("tensor", dve, dve_yev[-1])
                    if f >= 1:
                        emit_y(f - 1)
                emit_y(7)
                pe_y = pe.n
                ctx.wait("vector", pe, pe_y)
                for tcl in range(2):
                    tcg = strip * 2 + tcl
                    for oh2 in range(2):
                        accsl = ACC[:, tcg * 1024 + oh2 * 512: tcg * 1024 + oh2 * 512 + 512]
                        if e == 0:
                            ctx.op("vector", lambda v, tcl=tcl, oh2=oh2, accsl=accsl: v.scalar_tensor_tensor(
                                out=accsl, in0=PS[2 * tcl + oh2][:, 0:512],
                                scalar=1.0 / 3.0, in1=accsl, op0=OP.mult, op1=OP.add), dve, 1)
                        else:
                            wcol = wtm[:, tcg * 9 + e: tcg * 9 + e + 1]
                            ctx.op("vector", lambda v, tcl=tcl, oh2=oh2, accsl=accsl, wcol=wcol: v.scalar_tensor_tensor(
                                out=accsl, in0=PS[2 * tcl + oh2][:, 0:512],
                                scalar=wcol, in1=accsl, op0=OP.mult, op1=OP.add), dve, 1)
                dve_yev.append(dve.n)
            pe_sets.append(pe.n)
            if s + 2 < 18:
                ctx.wait("gpsimd", pe, pe_sets[-1])
                sp_sets.append(emit_set_load(s + 2))
            elif s == 16:
                ctx.wait("gpsimd", pe, pe_sets[-1])
                ctx.op("gpsimd", lambda g: g.dma_start(
                    out=wout_sb.rearrange("p (c n) -> p c n", n=1024),
                    in_=d_Wout.rearrange("(c p) n -> p c n", p=128)), sp, 16)
                sp_wout = sp.n
            elif s == 17:
                ctx.wait("gpsimd", pe, pe_sets[-1])
                ctx.op("gpsimd", lambda g: g.dma_start(
                    out=whg_sb.rearrange("p (c n) -> p c n", n=HID),
                    in_=d_Wh.rearrange("(c p) n -> p c n", p=128)[:, :, HID:2 * HID]), sp, 16)
                sp_whg = sp.n
        ctx.wait("gpsimd", dve, dve_yev[-1])
        dbg_dump("ACC_ffn", ACC)

        ctx.wait("gpsimd", cc, 1)
        for r in range(NCORES):
            ctx.op("gpsimd", lambda g, r=r: g.dma_start(out=ccr, in_=d_ccout[r * 128:(r + 1) * 128, :]), sp, 16)
            sp_ccr = sp.n
            ctx.wait("vector", sp, sp_ccr)
            if r == 0:
                ctx.op("vector", lambda v: v.tensor_scalar(
                    out=kvrun, in0=ccr, scalar1=maskb[:, 0:1], scalar2=None, op0=OP.mult), dve, 1)
            else:
                ctx.op("vector", lambda v, r=r: v.scalar_tensor_tensor(
                    out=kvrun, in0=ccr, scalar=maskb[:, r:r + 1], in1=kvrun,
                    op0=OP.mult, op1=OP.add), dve, 1)
            ctx.wait("gpsimd", dve, dve.n)
        dve_base = dve.n
        dbg_dump("kvbase", kvrun)

        sp_vld = []
        sp_nld = []

        def emit_group_loads(g):
            for i in range(2):
                tc = 2 * g + i
                ctx.op("gpsimd", lambda gp, g=g, tc=tc, i=i: gp.dma_start(
                    out=vring[:, ((2 * g + i) % 4) * HID: ((2 * g + i) % 4) * HID + HID],
                    in_=d_vsp[tc]), sp, 16)
                sp_vld.append(sp.n)
            ctx.op("gpsimd", lambda gp, g=g: gp.dma_start(
                out=normg[:, (g % 2) * KD * 256:(g % 2) * KD * 256 + KD * 256], in_=d_nsp[g]), sp, 16)
            sp_nld.append(sp.n)

        ctx.wait("gpsimd", sp, sp_whg)
        emit_group_loads(0)
        emit_group_loads(1)

        pe_bgrp = []
        sc_lkt2_last = None
        for g in range(NGRP):
            nrm = normg[:, (g % 2) * KD * 256:(g % 2) * KD * 256 + KD * 256]
            ctx.wait("tensor", sp, sp_nld[g])
            ctx.wait("tensor", sp, sp_vld[2 * g + 1])
            if g == 0:
                ctx.wait("tensor", sp, sp_wout)
                ctx.wait("tensor", sp, sp_whg)
                ctx.wait("tensor", sc, sc_bt)
                ctx.wait("tensor", dve, dve_yev[-1])
                ctx.wait("tensor", dve, dve_base)

            if g >= 1:
                ctx.wait("vector", pe, pe_bgrp[g - 1])
                ctx.wait("vector", sc, sc_lkt2_last)
            ctx.wait("vector", dve, dve_base)
            ctx.op("vector", lambda v: v.tensor_copy(out=kvbf, in_=kvrun), dve, 1)

            for c in range(KD):
                ctx.op("tensor", lambda t, c=c, nrm=nrm: t.matmul(
                    out=PS[4][:, 0:256], lhsT=wqk[:, c * 128:c * 128 + 128],
                    rhs=nrm[:, c * 256:c * 256 + 256],
                    start=(c == 0), stop=(c == KD - 1)), pe, 1)
            pe_qk2 = pe.n
            ctx.wait("scalar", pe, pe_qk2)
            ctx.op("scalar", lambda s: s.activation(out=qks, in_=PS[4][:, 0:256], func=AF.Silu, bias=bqk[:, 0:1]), act, 1)
            act_qk2 = act.n
            ctx.wait("vector", act, act_qk2)
            for (dst, hh) in ((quadq, 0), (linq, 1), (quadk, 2)):
                ctx.op("vector", lambda v, dst=dst, hh=hh: v.tensor_scalar(
                    out=dst, in0=qks, scalar1=qkw[:, hh:hh + 1], scalar2=qkb[:, hh:hh + 1],
                    op0=OP.mult, op1=OP.add))
            ctx.op("vector", lambda v: v.tensor_scalar(out=lkf, in0=qks, scalar1=qkw3s[:, 0:1], scalar2=qkb3s[:, 0:1], op0=OP.mult, op1=OP.add), dve, 1)
            dve_qk2 = dve.n
            ctx.wait("sync", dve, dve_qk2)
            if g >= 1:
                ctx.wait("sync", pe, pe_bgrp[g - 1])
            for i in range(2):
                ctx.op("sync", lambda e, i=i: e.dma_start(
                    out=linktm[:, i * 128:i * 128 + 128],
                    in_=lkf[:, i * 128:i * 128 + 128], transpose=True), sc, 16)
            sc_lkt2 = sc.n
            sc_lkt2_last = sc_lkt2

            ctx.wait("tensor", sc, sc_lkt2)
            kv_ev2 = []
            for hb in range(4):
                b = 5 + (hb % 2)
                if hb >= 2:
                    ctx.wait("tensor", dve, kv_ev2[hb - 2])
                for i in range(2):
                    vb = vring[:, ((2 * g + i) % 4) * HID: ((2 * g + i) % 4) * HID + HID]
                    ctx.op("tensor", lambda t, i=i, hb=hb, b=b, vb=vb: t.matmul(
                        out=PS[b][:, 0:512], lhsT=linktm[:, i * 128:i * 128 + 128],
                        rhs=vb[:, hb * 512:hb * 512 + 512],
                        start=(i == 0), stop=(i == 1)), pe, 1)
                pe_kv2 = pe.n
                ctx.wait("vector", pe, pe_kv2)
                ctx.op("vector", lambda v, hb=hb, b=b: v.tensor_add(
                    out=kvrun[:, hb * 512:hb * 512 + 512], in0=PS[b][:, 0:512],
                    in1=kvrun[:, hb * 512:hb * 512 + 512]), dve, 1)
                kv_ev2.append(dve.n)
            dve_kv2 = dve.n

            ctx.wait("tensor", dve, dve_kv2)
            for kt in range(2):
                ctx.op("tensor", lambda t, kt=kt: t.matmul(
                    out=PS[4 + kt][:, 0:256], lhsT=quadk[:, kt * 128:kt * 128 + 128],
                    rhs=quadq, start=True, stop=True), pe, 1)
            pe_sim = pe.n
            ctx.wait("vector", pe, pe_sim)
            if g == 0:
                ctx.wait("vector", sc, sc_bt)
            for kt in range(2):
                ctx.op("vector", lambda v, kt=kt: v.scalar_tensor_tensor(
                    out=simy[:, kt * 256:kt * 256 + 256], in0=PS[4 + kt][:, 0:256],
                    scalar=1.0 / GRP, in1=BT[:, kt * 256:kt * 256 + 256],
                    op0=OP.mult, op1=OP.add))
            ctx.op("vector", lambda v: v.tensor_scalar(out=simy, in0=simy, scalar1=0.0, scalar2=None, op0=OP.max))
            ctx.op("vector", lambda v: v.tensor_mul(out=attnT, in0=simy, in1=simy), dve, 1)
            dve_attnT = dve.n
            if g == 0:
                ctx.wait("gpsimd", dve, dve_attnT)
                dbg_dump("attnT0", attnT)

            act_gate_l = []
            dve_outh_l = []
            pe_wout_l = []
            for hc in range(NHC):
                gb = 4 + (hc % 2)
                qb = 6 + (hc % 2)
                gsl = gate_r[:, (hc % 2) * 256:(hc % 2) * 256 + 256]
                osl = outh_r[:, (hc % 2) * 256:(hc % 2) * 256 + 256]
                if hc < 2:
                    ctx.wait("tensor", dve, dve_attnT)
                else:
                    ctx.wait("tensor", act, act_gate_l[hc - 2])
                for c in range(KD):
                    ctx.op("tensor", lambda t, c=c, hc=hc, gb=gb, nrm=nrm: t.matmul(
                        out=PS[gb][:, 0:256],
                        lhsT=whg_sb[:, c * HID + hc * 128: c * HID + hc * 128 + 128],
                        rhs=nrm[:, c * 256:c * 256 + 256],
                        start=(c == 0), stop=(c == KD - 1)), pe, 1)
                pe_gate = pe.n
                if hc < 2:
                    ctx.wait("tensor", dve, dve_kv2)
                else:
                    ctx.wait("tensor", dve, dve_outh_l[hc - 2])
                for kt in range(2):
                    vb = vring[:, ((2 * g + kt) % 4) * HID: ((2 * g + kt) % 4) * HID + HID]
                    ctx.op("tensor", lambda t, kt=kt, hc=hc, qb=qb, vb=vb: t.matmul(
                        out=PS[qb][:, 0:256], lhsT=vb[:, hc * 128:hc * 128 + 128],
                        rhs=attnT[:, kt * 256:kt * 256 + 256],
                        start=(kt == 0), stop=False), pe, 1)
                ctx.op("tensor", lambda t, hc=hc, qb=qb: t.matmul(
                    out=PS[qb][:, 0:256], lhsT=kvbf[:, hc * 128:hc * 128 + 128],
                    rhs=linq, start=False, stop=True), pe, 1)
                pe_ql = pe.n
                ctx.wait("scalar", pe, pe_gate)
                if hc >= 2:
                    ctx.wait("scalar", dve, dve_outh_l[hc - 2])
                ctx.op("scalar", lambda s, hc=hc, gb=gb, gsl=gsl: s.activation(
                    out=gsl, in_=PS[gb][:, 0:256], func=AF.Silu, bias=bhg[:, hc:hc + 1]), act, 1)
                act_gate_l.append(act.n)
                ctx.wait("vector", pe, pe_ql)
                ctx.wait("vector", act, act_gate_l[-1])
                if hc >= 2:
                    ctx.wait("vector", pe, pe_wout_l[hc - 2])
                ctx.op("vector", lambda v, qb=qb, gsl=gsl, osl=osl: v.tensor_mul(
                    out=osl, in0=gsl, in1=PS[qb][:, 0:256]), dve, 1)
                dve_outh_l.append(dve.n)
                ctx.wait("tensor", dve, dve_outh_l[-1])
                for tcl in range(2):
                    for oh2 in range(2):
                        ctx.op("tensor", lambda t, hc=hc, tcl=tcl, oh2=oh2, osl=osl: t.matmul(
                            out=PS[2 * tcl + oh2][:, 0:512],
                            lhsT=osl[:, tcl * 128:tcl * 128 + 128],
                            rhs=wout_sb[:, hc * 1024 + oh2 * 512: hc * 1024 + oh2 * 512 + 512],
                            start=(hc == 0), stop=(hc == NHC - 1)), pe, 1)
                pe_wout_l.append(pe.n)

            pe_grp = pe.n
            pe_bgrp.append(pe_grp)
            ctx.wait("vector", pe, pe_grp)
            for tcl in range(2):
                tcg = 2 * g + tcl
                for oh2 in range(2):
                    accsl = ACC[:, tcg * 1024 + oh2 * 512: tcg * 1024 + oh2 * 512 + 512]
                    ctx.op("vector", lambda v, tcl=tcl, oh2=oh2, accsl=accsl: v.tensor_add(
                        out=accsl, in0=PS[2 * tcl + oh2][:, 0:512], in1=accsl), dve, 1)
            dve_grp = dve.n

            if g + 2 < NGRP:
                ctx.wait("gpsimd", dve, dve_grp)
                emit_group_loads(g + 2)

            ctx.wait("sync", dve, dve_grp)
            for tcl in range(2):
                tcg = 2 * g + tcl
                ctx.op("sync", lambda e, tcg=tcg: e.dma_start(
                    out=d_out[tcg * 128:(tcg + 1) * 128, :],
                    in_=ACC[:, tcg * 1024:(tcg + 1) * 1024]), out_s, 16)

        if "ACC_fin" in d_dbg:
            ctx.wait("gpsimd", dve, dve.n)
            dbg_dump("ACC_fin", ACC)

        ctx.wait("sync", out_s, out_s.n)
        ctx.wait("sync", sp, sp.n)

    block.sync(lambda e: program(Ctx("sync", e)))
    block.scalar(lambda e: program(Ctx("scalar", e)))
    block.vector(lambda e: program(Ctx("vector", e)))
    block.tensor(lambda e: program(Ctx("tensor", e)))
    block.gpsimd(lambda e: program(Ctx("gpsimd", e)))

    es.close()
    return nc


def kernel(**inputs):
    oh = _host_consts()
    xf = np.ascontiguousarray(np.asarray(inputs["L0_attn_x"], dtype=np.float32).reshape(16384, DIM))
    base = {
        "ln_w": np.ascontiguousarray(inputs["L0_attn_ln_w"], dtype=np.float32),
        "ln_b": np.ascontiguousarray(inputs["L0_attn_ln_b"], dtype=np.float32),
        "Wh": np.ascontiguousarray(inputs["L0_attn_Wh"], dtype=np.float32),
        "bh": np.ascontiguousarray(inputs["L0_attn_bh"], dtype=np.float32),
        "Wqk": np.ascontiguousarray(inputs["L0_attn_Wqk"], dtype=np.float32),
        "bqk": np.ascontiguousarray(inputs["L0_attn_bqk"], dtype=np.float32),
        "qk_w": np.ascontiguousarray(inputs["L0_attn_qk_w"], dtype=np.float32),
        "qk_b": np.ascontiguousarray(inputs["L0_attn_qk_b"], dtype=np.float32),
        "rel_table": np.ascontiguousarray(inputs["L0_attn_rel_table"], dtype=np.float32),
        "Wout": np.ascontiguousarray(inputs["L0_attn_Wout"], dtype=np.float32),
        "bout": np.ascontiguousarray(inputs["L0_attn_bout"], dtype=np.float32),
        "Wg": np.ascontiguousarray(inputs["L0_ffn_Wg"], dtype=np.float32),
        "Ws1": np.ascontiguousarray(inputs["L0_ffn_Ws1"], dtype=np.float32),
        "Ws3": np.ascontiguousarray(inputs["L0_ffn_Ws3"], dtype=np.float32),
        "Ws2": np.ascontiguousarray(inputs["L0_ffn_Ws2"], dtype=np.float32),
        "We1": np.ascontiguousarray(inputs["L0_ffn_We1"], dtype=np.float32),
        "We3": np.ascontiguousarray(inputs["L0_ffn_We3"], dtype=np.float32),
        "We2": np.ascontiguousarray(inputs["L0_ffn_We2"], dtype=np.float32),
        "ohm": oh,
    }
    in_maps = []
    for c in range(NCORES):
        m = dict(base)
        m["x"] = np.ascontiguousarray(xf[c * TOK:(c + 1) * TOK])
        sm = np.zeros((1, 8), dtype=np.float32)
        for r in range(NCORES):
            if r // 4 == c // 4 and r < c:
                sm[0, r] = 1.0
        m["smask"] = sm
        in_maps.append(m)

    nc = build_program(n_debug=tuple(DEBUG.keys()))
    res = run_bass_kernel_spmd(nc, in_maps, core_ids=list(range(NCORES)))
    out = np.concatenate([np.asarray(r["out"]) for r in res.results], axis=0)
    if DEBUG:
        kernel.debug_results = res.results
    return out.reshape(2, 8192, DIM)


if __name__ == "__main__":
    nc = build_program()
    print("built ok")
